# version 14
# speedup vs baseline: 1.1715x; 1.0360x over previous
import numpy as np
import ml_dtypes
from contextlib import ExitStack

import concourse.bass as bass
import concourse.mybir as mybir
from concourse.bass_utils import run_bass_kernel_spmd

F32 = mybir.dt.float32
BF16 = mybir.dt.bfloat16
U8 = mybir.dt.uint8
AX = mybir.AxisListType
ALU = mybir.AluOpType
ACT = mybir.ActivationFunctionType

D = 1024
NH = 16
HD = 64
D3 = 3 * D
INNER = 4096
NST = 128
HC = 512
G3 = 3 * HC
NM = G3 // 128
LN_EPS = 1e-5
NCORES = 8

ENGMAP = {"t": "tensor", "v": "vector", "a": "scalar", "s": "sync", "g": "gpsimd"}


def nsplit(total, step):
    return [(i, min(step, total - i)) for i in range(0, total, step)]


class Prog:

    NPOOL = {"sd": 10, "ad": 10, "gd": 2}

    def __init__(self, nc, ctx):
        self.nc = nc
        self.keys = ["t", "v", "a"]
        for base, n in self.NPOOL.items():
            self.keys += [f"{base}{i}" for i in range(n)]
        self.sem = {k: ctx.enter_context(nc.semaphore("tk_" + k)) for k in self.keys}
        self.cnt = {k: 0 for k in self.keys}
        self.rot = {"sd": 0, "ad": 0, "gd": 0}
        self.wm = {e: {k: 0 for k in self.keys} for e in ENGMAP}
        self.q = {e: [] for e in ENGMAP}
        self.blk = None

    def barrier(self):
        for e, base in [("s", "sd"), ("a", "ad"), ("g", "gd")]:
            deps = [(f"{base}{i}", self.cnt[f"{base}{i}"]) for i in range(self.NPOOL[base])]
            if any(v > self.wm[e][k] for k, v in deps):
                self.wait(e, deps)
        self.flush()
        self.nc.all_engine_barrier()
        for e in ENGMAP:
            for k in self.keys:
                self.wm[e][k] = self.cnt[k]

    def _waits(self, eng, e, deps):
        best = {}
        for d in deps:
            if d is None:
                continue
            k, v = d
            if v > best.get(k, 0):
                best[k] = v
        for k, v in sorted(best.items()):
            if v > self.wm[e][k]:
                eng.wait_ge(self.sem[k], v)
                self.wm[e][k] = v

    def _calc_waits(self, e, deps):
        best = {}
        for d in deps:
            if d is None:
                continue
            k, v = d
            if v > best.get(k, 0):
                best[k] = v
        out = []
        for k, v in sorted(best.items()):
            if v > self.wm[e][k]:
                out.append((k, v))
                self.wm[e][k] = v
        return out

    def run(self, e, fn, deps=()):
        semkey = e
        waits = self._calc_waits(e, deps)
        self.cnt[semkey] += 1
        tick = (semkey, self.cnt[semkey])

        def play(eng, waits=waits, fn=fn, semkey=semkey):
            for (k, v) in waits:
                eng.wait_ge(self.sem[k], v)
            last = fn(eng)
            assert last is not None
            last.then_inc(self.sem[semkey], 1)

        self.q[e].append(play)
        return tick

    def dma(self, e, out, in_, deps=(), transpose=False):
        base = {"s": "sd", "a": "ad", "g": "gd"}[e]
        k = self.rot[base]
        self.rot[base] = (k + 1) % self.NPOOL[base]
        semkey = f"{base}{k}"
        prev = (semkey, self.cnt[semkey])
        waits = self._calc_waits(e, list(deps) + [prev])
        self.cnt[semkey] += 16
        tick = (semkey, self.cnt[semkey])

        def play(eng, waits=waits, out=out, in_=in_, transpose=transpose, semkey=semkey):
            for (kk, v) in waits:
                eng.wait_ge(self.sem[kk], v)
            if transpose:
                instr = eng.dma_start_transpose(out, in_)
            else:
                instr = eng.dma_start(out, in_)
            instr.then_inc(self.sem[semkey], 16)

        self.q[e].append(play)
        return tick

    def wait(self, e, deps):
        waits = self._calc_waits(e, deps)

        def play(eng, waits=waits):
            for (k, v) in waits:
                eng.wait_ge(self.sem[k], v)

        self.q[e].append(play)

    def flush(self):
        if not any(self.q.values()):
            return
        qs = {e: lst for e, lst in self.q.items() if lst}
        self.q = {e: [] for e in ENGMAP}
        with self.nc.Block() as blk:
            for e, lst in qs.items():
                def body(eng, lst=lst):
                    for play in lst:
                        play(eng)
                getattr(blk, ENGMAP[e])(body)


def col1(ap_1d, lo, hi):
    return ap_1d[lo:hi].rearrange("(p o) -> p o", o=1)


def build(SEQ, T):
    nc = bass.Bass("TRN2", target_bir_lowering=False)
    _uid = [0]

    def U(n):
        _uid[0] += 1
        return f"{n}_{_uid[0]}"
    TT = T // 128
    S4 = 4 * SEQ

    din = {}
    for name, shape, dt in [
        ("L0_attn_x", [SEQ, T, D], F32),
        ("L0_attn_W_qkv", [D, D3], F32), ("L0_attn_b_qkv", [D3], F32),
        ("L0_attn_W_proj", [D, D], F32), ("L0_attn_b_proj", [D], F32),
        ("L0_ffn_w_gate", [INNER, D], F32), ("L0_ffn_w_up", [INNER, D], F32),
        ("L0_ffn_w_down", [D, INNER], F32),
        ("L1_attn_A_log", [NST], F32), ("L1_attn_B_w", [NST, D], F32),
        ("L1_attn_B_b", [NST], F32), ("L1_attn_C_w", [D, NST], F32),
        ("L1_attn_C_b", [D], F32), ("L1_attn_norm_g", [D], F32),
        ("L1_attn_norm_b", [D], F32),
        ("L1_attn_ff1_w", [HC, D + HC], F32), ("L1_attn_ff1_b", [HC], F32),
        ("L1_attn_ff2_w", [HC, D + HC], F32), ("L1_attn_ff2_b", [HC], F32),
        ("L1_attn_ta_w", [HC, D + HC], F32), ("L1_attn_ta_b", [HC], F32),
        ("L1_attn_tb_w", [HC, D + HC], F32), ("L1_attn_tb_b", [HC], F32),
        ("L1_attn_out_w", [D, HC], F32), ("L1_attn_out_b", [D], F32),
        ("L1_ffn_k0", [2 * D, INNER], U8), ("L1_ffn_k1", [INNER, INNER], U8),
        ("L1_ffn_k2", [INNER, D], U8),
        ("L1_ffn_s0", [1, 1], F32), ("L1_ffn_s1", [1, 1], F32), ("L1_ffn_s2", [1, 1], F32),
    ]:
        din[name] = nc.dram_tensor(name, shape, dt, kind="ExternalInput")
    out_h = nc.dram_tensor("out", [SEQ, T, D], F32, kind="ExternalOutput")

    ds = {}
    for name, shape, dt in [
        ("wqkv_b", [8, 128, D3], BF16), ("wproj_b", [8, 128, D], BF16),
        ("wqn_g", [32, 128, D], BF16), ("wqn_u", [32, 128, D], BF16),
        ("wqn_d", [8, 128, INNER], BF16),
        ("wqT_g", [8, 128, INNER], BF16), ("wqT_u", [8, 128, INNER], BF16),
        ("wqT_d", [32, 128, D], BF16),
        ("A0_s", [8, 128, INNER], BF16), ("A1_s", [32, 128, INNER], BF16),
        ("A2_s", [32, 128, D], BF16),
        ("PxT_s", [8, 128, G3], BF16), ("VhT_s", [4, 128, G3], BF16),
        ("outwT_s", [4, 128, D], BF16),
        ("q_s", [SEQ, 8, 128, T], BF16), ("k_s", [SEQ, 8, 128, T], BF16),
        ("v_s", [SEQ, TT, 128, NH * (HD + 1)], BF16),
        ("s0_s", [SEQ, 8, 128, T], F32),
        ("x1_s", [SEQ, T, D], F32),
        ("xn1T_s", [SEQ, 8, 128, T], BF16),
        ("xniT_s", [SEQ, 8, 128, T], BF16),
        ("P_s", [128, NM, SEQ, T], F32),
        ("apow_d", [128, T], F32), ("asafe_d", [128, T], F32),
        ("cfc_s", [128, T * S4], BF16),
        ("uT_s", [SEQ, T, D], F32),
        ("g1T_s", [SEQ, T, D], F32),
    ]:
        ds[name] = nc.dram_tensor(name, shape, dt, kind="Internal")

    eye_b_h = nc.inline_tensor(np.eye(128).astype(ml_dtypes.bfloat16), name="c_eye_b")
    eye_f_h = nc.inline_tensor(np.eye(128, dtype=np.float32), name="c_eye_f")
    ones64_h = nc.inline_tensor(np.ones((64, 1)).astype(ml_dtypes.bfloat16), name="c_ones64")
    ones1_h = nc.inline_tensor(np.ones((1, 128), dtype=np.float32), name="c_ones1")
    trow_h = nc.inline_tensor(np.arange(T, dtype=np.float32)[None, :], name="c_trow")

    ctx = ExitStack()
    SB = lambda n, s, d=F32: ctx.enter_context(nc.sbuf_tensor(n, s, d))

    eye_b = SB("eye_b", [128, 128], BF16)
    eye_f = SB("eye_f", [128, 128], F32)
    ones64 = SB("ones64", [64, 1], BF16)
    ones1 = SB("ones1", [1, 128], F32)
    bqkv_sb = SB("bqkv_sb", [128, 16])
    bvbc = SB("bvbc", [128, D])
    bproj_sb = SB("bproj_sb", [128, 8])
    Bb_sb = SB("Bb_sb", [128, 1])
    Cb_sb = SB("Cb_sb", [128, 8])
    outb_sb = SB("outb_sb", [128, 8])
    gam_sb = SB("gam_sb", [128, 8])
    bet_sb = SB("bet_sb", [128, 8])
    Pb_sb = SB("Pb_sb", [128, NM])
    lnA_sb = SB("lnA_sb", [128, 1])
    rsw_sb = SB("rsw_sb", [128, 5])
    sc_sb = SB("sc_sb", [128, 2])
    BwT_sb = SB("BwT_sb", [128, 8 * 128], BF16)
    CwT_sb = SB("CwT_sb", [128, 8 * 128], BF16)
    inf_sb = SB("inf_sb", [1, SEQ])
    h0_sb = SB("h0_sb", [128, S4], BF16)
    half_sb = SB("half_sb", [128, 1])

    P = Prog(nc, ctx)

    with ExitStack() as px:
        XS = lambda n, s, d=F32: px.enter_context(nc.sbuf_tensor(n, s, d))
        tload = []
        for sb_t, h in [(eye_b, eye_b_h), (eye_f, eye_f_h), (ones64, ones64_h),
                        (ones1, ones1_h)]:
            tload.append(P.dma("s", sb_t[:], h.ap()))
        t_ones = tload[3]
        bq = din["L0_attn_b_qkv"].ap()
        for mt in range(16):
            P.dma("s", bqkv_sb[:, mt:mt + 1], col1(bq, mt * 128, (mt + 1) * 128))
        for mt in range(8):
            P.dma("s", bproj_sb[:, mt:mt + 1], col1(din["L0_attn_b_proj"].ap(), mt * 128, (mt + 1) * 128))
            P.dma("s", Cb_sb[:, mt:mt + 1], col1(din["L1_attn_C_b"].ap(), mt * 128, (mt + 1) * 128))
            P.dma("s", outb_sb[:, mt:mt + 1], col1(din["L1_attn_out_b"].ap(), mt * 128, (mt + 1) * 128))
            P.dma("s", gam_sb[:, mt:mt + 1], col1(din["L1_attn_norm_g"].ap(), mt * 128, (mt + 1) * 128))
            P.dma("s", bet_sb[:, mt:mt + 1], col1(din["L1_attn_norm_b"].ap(), mt * 128, (mt + 1) * 128))
        P.dma("s", Bb_sb[:, 0:1], col1(din["L1_attn_B_b"].ap(), 0, NST))
        tA = P.dma("s", lnA_sb[:, 0:1], col1(din["L1_attn_A_log"].ap(), 0, NST))
        for j in range(4):
            P.dma("s", Pb_sb[:, j:j + 1], col1(din["L1_attn_ff1_b"].ap(), j * 128, (j + 1) * 128))
            P.dma("s", Pb_sb[:, 4 + j:5 + j], col1(din["L1_attn_ff2_b"].ap(), j * 128, (j + 1) * 128))
        tmp4 = XS("w0_tmp4", [128, 4])
        t_ta = [P.dma("s", tmp4[:, j:j + 1], col1(din["L1_attn_ta_b"].ap(), j * 128, (j + 1) * 128)) for j in range(4)]
        t_tb = [P.dma("s", Pb_sb[:, 8 + j:9 + j], col1(din["L1_attn_tb_b"].ap(), j * 128, (j + 1) * 128)) for j in range(4)]
        P.run("v", lambda v: [v.tensor_single_scalar(Pb_sb[:, 8:12], Pb_sb[:, 8:12], 0.5, ALU.mult),
                              v.drain(),
                              v.scalar_tensor_tensor(Pb_sb[:, 8:12], tmp4[:], 0.05, Pb_sb[:, 8:12],
                                                     ALU.mult, ALU.add)][-1], deps=t_ta + t_tb)
        P.run("v", lambda v: v.memset(h0_sb[:], 0.0))
        P.run("v", lambda v: v.memset(half_sb[:], 0.5))
        t2 = P.run("a", lambda a: a.activation(lnA_sb[:, 0:1], lnA_sb[:, 0:1], ACT.Sigmoid), deps=[tA])
        t3 = P.run("a", lambda a: a.activation(lnA_sb[:, 0:1], lnA_sb[:, 0:1], ACT.Ln), deps=[t2])
        bv1 = XS("w0_bv", [1, D])
        tbv = P.dma("s", bv1[:], bq[2 * D:3 * D].rearrange("(o f) -> o f", o=1))
        ps_b = px.enter_context(nc.psum_tensor(U("w0_ps"), [128, D], F32))
        tms = [P.run("t", lambda t, n0=n0, nn=nn: t.matmul(ps_b[:, n0:n0 + nn], ones1[:],
                                                           bv1[:, n0:n0 + nn], start=True, stop=True),
                     deps=[tbv, t_ones]) for n0, nn in nsplit(D, 512)]
        P.run("v", lambda v: v.tensor_copy(bvbc[:], ps_b[:]), deps=tms)
        trow_sb = XS("w0_trow", [1, T])
        s3 = XS("w0_s3", [1, 3])
        apw = XS("w0_apw", [128, T])
        ttr = P.dma("s", trow_sb[:], trow_h.ap())
        tss = [P.dma("s", s3[:, i:i + 1], din[f"L1_ffn_s{i}"].ap()) for i in range(3)]
        tsm = P.run("v", lambda v: v.tensor_scalar_mul(s3[:], s3[:], 2.0), deps=tss)
        ps_b2 = px.enter_context(nc.psum_tensor(U("w0_ps2"), [128, T], F32))
        ps_s3 = px.enter_context(nc.psum_tensor(U("w0_ps3"), [128, 3], F32))
        tms = [P.run("t", lambda t, n0=n0, nn=nn: t.matmul(ps_b2[:, n0:n0 + nn], ones1[:],
                                                           trow_sb[:, n0:n0 + nn], start=True, stop=True),
                     deps=[ttr, t_ones]) for n0, nn in nsplit(T, 512)]
        tsp = P.run("t", lambda t: t.matmul(ps_s3[:], ones1[:], s3[:], start=True, stop=True), deps=[tsm])
        tap = P.run("a", lambda a: a.activation(apw[:], ps_b2[:], ACT.Exp, scale=lnA_sb[:, 0:1]),
                    deps=tms + [t3])
        P.run("v", lambda v: v.tensor_copy(sc_sb[:], ps_s3[:, 0:2]), deps=[tsp])
        P.run("v", lambda v: v.tensor_copy(rsw_sb[:, 3:4], ps_s3[:, 2:3]), deps=[tsp])
        tap2 = P.dma("s", ds["apow_d"].ap(), apw[:], deps=[tap])
        tmx = P.run("v", lambda v: v.tensor_scalar_max(apw[:], apw[:], 1e-8), deps=[tap, tap2])
        P.dma("s", ds["asafe_d"].ap(), apw[:], deps=[tmx])
    P.barrier()

    with ExitStack() as px:
        wb = [px.enter_context(nc.sbuf_tensor(U(f"w1_wb{i}"), [128, D3], F32)) for i in range(2)]
        wc = [px.enter_context(nc.sbuf_tensor(U(f"w1_wc{i}"), [128, D3], BF16)) for i in range(2)]
        jobs = [("L0_attn_W_qkv", "wqkv_b", D3, kc) for kc in range(8)] + \
               [("L0_attn_W_proj", "wproj_b", D, kc) for kc in range(8)]
        tv = {}
        tst = {}
        for ji, (src, dstn, ncol, kc) in enumerate(jobs):
            tl = P.dma("s", wb[ji % 2][:, 0:ncol], din[src].ap()[kc * 128:(kc + 1) * 128, :],
                       deps=[tv.get(ji - 2)])
            tv[ji] = P.run("v", lambda v, ji=ji, ncol=ncol: v.tensor_copy(wc[ji % 2][:, 0:ncol],
                                                                          wb[ji % 2][:, 0:ncol]),
                           deps=[tl, tst.get(ji - 2)])
            tst[ji] = P.dma("a", ds[dstn].ap()[kc], wc[ji % 2][:, 0:ncol], deps=[tv[ji]])
    P.barrier()

    for wname, spill, nchunks, ncols, rcol in [
        ("L0_ffn_w_gate", "wqn_g", 32, D, 0),
        ("L0_ffn_w_up", "wqn_u", 32, D, 1),
        ("L0_ffn_w_down", "wqn_d", 8, INNER, 2),
    ]:
        with ExitStack() as px:
            wb = [px.enter_context(nc.sbuf_tensor(U(f"w2_wb{i}"), [128, ncols], F32)) for i in range(2)]
            red = px.enter_context(nc.sbuf_tensor(U("w2_red"), [128, 1], F32))
            acc = px.enter_context(nc.sbuf_tensor(U("w2_acc"), [128, 1], F32))
            t1s = px.enter_context(nc.sbuf_tensor(U("w2_t1"), [1, 1], F32))
            gbit = px.enter_context(nc.sbuf_tensor(U("w2_g"), [128, ncols], F32))
            sgn = [px.enter_context(nc.sbuf_tensor(U(f"w2_s{i}"), [128, ncols], F32)) for i in range(2)]
            wq = [px.enter_context(nc.sbuf_tensor(U(f"w2_wq{i}"), [128, ncols], BF16)) for i in range(2)]
            swv = px.enter_context(nc.sbuf_tensor(U("w2_sw"), [128, 1], F32))
            ps_t = px.enter_context(nc.psum_tensor(U("w2_ps"), [1, 128], F32))
            ps_sp = px.enter_context(nc.psum_tensor(U("w2_ps2"), [128, 1], F32))
            tred = P.run("v", lambda v: v.memset(acc[:], 0.0))
            for c in range(nchunks):
                tl = P.dma("s", wb[c % 2][:], din[wname].ap()[c * 128:(c + 1) * 128, :],
                           deps=[tred] if c >= 2 else [])
                tr = P.run("v", lambda v, c=c: v.reduce_sum(red[:], wb[c % 2][:], AX.X,
                                                            apply_absolute_value=True), deps=[tl, tred])
                tred = P.run("v", lambda v: v.tensor_add(acc[:], acc[:], red[:]), deps=[tr])
            ttr = P.run("t", lambda t: t.transpose(ps_t[:], acc[:, 0:1], eye_f[:]), deps=[tred])
            tt1 = P.run("v", lambda v: v.reduce_sum(t1s[:], ps_t[0:1, :], AX.X), deps=[ttr])
            tt2 = P.run("v", lambda v: v.tensor_scalar(t1s[:], t1s[:], 1.0 / (nchunks * 128 * ncols),
                                                       1e-5, ALU.mult, ALU.max), deps=[tt1])
            tsp = P.run("t", lambda t: t.matmul(ps_sp[:], ones1[:], t1s[:], start=True, stop=True),
                        deps=[tt2])
            tsw = P.run("v", lambda v: v.reciprocal(swv[:], ps_sp[:]), deps=[tsp])
            P.run("v", lambda v: v.tensor_copy(rsw_sb[:, rcol:rcol + 1], ps_sp[:]), deps=[tsp])
            tst = {}
            tm = None
            for c in range(nchunks):
                tl = P.dma("s", wb[c % 2][:], din[wname].ap()[c * 128:(c + 1) * 128, :],
                           deps=[tst.get(c - 2), tsw])
                tsg = P.run("a", lambda a, c=c: a.activation(sgn[c % 2][:], wb[c % 2][:], ACT.Sign),
                            deps=[tl])
                tg0 = P.run("a", lambda a, c=c: a.activation(gbit[:], wb[c % 2][:], ACT.Abs),
                            deps=[tl, tm])
                tg = P.run("v", lambda v: v.tensor_scalar_mul(gbit[:], gbit[:], swv[:, 0:1]),
                           deps=[tg0])
                tg2 = P.run("v", lambda v: v.tensor_single_scalar(gbit[:], gbit[:], 0.5, ALU.is_ge),
                            deps=[tg])
                tm = P.run("v", lambda v, c=c: v.tensor_mul(wq[c % 2][:], gbit[:], sgn[c % 2][:]),
                           deps=[tg2, tsg, tst.get(c - 2)])
                tst[c] = P.dma("a", ds[spill].ap()[c], wq[c % 2][:], deps=[tm])
        P.barrier()
    P.run("v", lambda v: v.tensor_mul(rsw_sb[:, 4:5], rsw_sb[:, 1:2], rsw_sb[:, 2:3]))

    with ExitStack() as px:
        tb = [px.enter_context(nc.sbuf_tensor(U(f"w3_tb{i}"), [128, INNER], BF16)) for i in range(2)]
        prev = {}
        i = 0
        for src, dstn, nj, ncols in [("wqn_g", "wqT_g", 8, INNER), ("wqn_u", "wqT_u", 8, INNER),
                                     ("wqn_d", "wqT_d", 32, D)]:
            for j in range(nj):
                tl = P.dma("s", tb[i % 2][:, 0:ncols],
                           ds[src].ap()[:, :, j * 128:(j + 1) * 128].rearrange("a b c -> (a b) c"),
                           deps=[prev.get(i - 2)], transpose=True)
                prev[i] = P.dma("a", ds[dstn].ap()[j], tb[i % 2][:, 0:ncols], deps=[tl])
                i += 1
    P.barrier()

    with ExitStack() as px:
        ub = [px.enter_context(nc.sbuf_tensor(U(f"w4_u{i}"), [128, INNER], U8)) for i in range(2)]
        ub2 = [px.enter_context(nc.sbuf_tensor(U(f"w4_u2{i}"), [128, INNER], U8)) for i in range(2)]
        cb = [px.enter_context(nc.sbuf_tensor(U(f"w4_c{i}"), [128, INNER], BF16)) for i in range(2)]
        tcv = {}
        tstv = {}
        i = 0
        for c in range(8):
            tl = P.dma("s", ub[i % 2][:], din["L1_ffn_k0"].ap()[c * 128:(c + 1) * 128, :],
                       deps=[tcv.get(i - 2)])
            tl2 = P.dma("s", ub2[i % 2][:], din["L1_ffn_k0"].ap()[D + c * 128:D + (c + 1) * 128, :],
                        deps=[tcv.get(i - 2)])
            tcv[i] = P.run("v", lambda v, i=i: v.tensor_sub(cb[i % 2][:], ub[i % 2][:], ub2[i % 2][:]),
                           deps=[tl, tl2, tstv.get(i - 2)])
            tstv[i] = P.dma("a", ds["A0_s"].ap()[c], cb[i % 2][:], deps=[tcv[i]])
            i += 1
        for src, dstn, nchunks, ncols in [("L1_ffn_k1", "A1_s", 32, INNER), ("L1_ffn_k2", "A2_s", 32, D)]:
            for c in range(nchunks):
                tl = P.dma("s", ub[i % 2][:, 0:ncols], din[src].ap()[c * 128:(c + 1) * 128, :],
                           deps=[tcv.get(i - 2)])
                tcv[i] = P.run("v", lambda v, i=i, ncols=ncols: v.tensor_scalar(
                    cb[i % 2][:, 0:ncols], ub[i % 2][:, 0:ncols], 2.0, -1.0, ALU.mult, ALU.add),
                    deps=[tl, tstv.get(i - 2)])
                tstv[i] = P.dma("a", ds[dstn].ap()[c], cb[i % 2][:, 0:ncols], deps=[tcv[i]])
                i += 1
    P.barrier()

    with ExitStack() as px:
        wb = px.enter_context(nc.sbuf_tensor(U("w5_wb"), [128, D + HC], F32))
        wb2 = px.enter_context(nc.sbuf_tensor(U("w5_wb2"), [128, D + HC], F32))
        cbw = px.enter_context(nc.sbuf_tensor(U("w5_cb"), [128, D + HC], BF16))
        stg = [px.enter_context(nc.sbuf_tensor(U(f"w5_st{i}"), [128, 128], BF16)) for i in range(2)]
        ps_t = [px.enter_context(nc.psum_tensor(U(f"w5_ps{i}"), [128, 128], BF16)) for i in range(2)]
        jobs = [("B_w", "L1_attn_B_w", 0, 0, D, "BwT")]
        jobs += [("C_w", "L1_attn_C_w", c, 0, NST, "CwT") for c in range(8)]
        for g, wname in enumerate(["L1_attn_ff1_w", "L1_attn_ff2_w", "VAB"]):
            jobs += [(wname, wname, c, g, D + HC, "PxVh") for c in range(4)]
        jobs += [("out_w", "L1_attn_out_w", c, 0, HC, "outwT") for c in range(8)]
        tcast = None
        tevac = {}
        u = 0
        for ji, (nm, wname, c, g, ncols, kind) in enumerate(jobs):
            ntile = (ncols + 127) // 128
            prev_trs = [tevac.get(u - jj) for jj in range(1, ntile + 1)]
            if nm == "VAB":
                tl1 = P.dma("s", wb[:, 0:ncols], din["L1_attn_ta_w"].ap()[c * 128:(c + 1) * 128, :],
                            deps=[tcast])
                tl2 = P.dma("s", wb2[:, 0:ncols], din["L1_attn_tb_w"].ap()[c * 128:(c + 1) * 128, :],
                            deps=[tcast])
                tl = P.run("v", lambda v, ncols=ncols: [
                    v.tensor_single_scalar(wb2[:, 0:ncols], wb2[:, 0:ncols], 0.5, ALU.mult),
                    v.drain(),
                    v.scalar_tensor_tensor(wb[:, 0:ncols], wb[:, 0:ncols], 0.05, wb2[:, 0:ncols],
                                           ALU.mult, ALU.add)][-1],
                    deps=[tl1, tl2])
            else:
                tl = P.dma("s", wb[:, 0:ncols], din[wname].ap()[c * 128:(c + 1) * 128, :], deps=[tcast])
            tcast = P.run("v", lambda v, ncols=ncols: v.tensor_copy(cbw[:, 0:ncols], wb[:, 0:ncols]),
                          deps=[tl] + prev_trs)
            for jt in range(ntile):
                w = min(128, ncols - jt * 128)
                ttr = P.run("t", lambda t, jt=jt, w=w, u=u: t.transpose(
                    ps_t[u % 2][:, 0:128], cbw[:, jt * 128:jt * 128 + w], eye_b[:]),
                    deps=[tcast, tevac.get(u - 2)])
                if kind == "BwT":
                    tevac[u] = P.run("v", lambda v, jt=jt, w=w, u=u: v.tensor_copy(
                        BwT_sb[0:w, jt * 128:(jt + 1) * 128], ps_t[u % 2][0:w, 0:128]), deps=[ttr])
                elif kind == "CwT":
                    tevac[u] = P.run("v", lambda v, c=c, w=w, u=u: v.tensor_copy(
                        CwT_sb[0:w, c * 128:(c + 1) * 128], ps_t[u % 2][0:w, 0:128]), deps=[ttr])
                else:
                    te = P.run("v", lambda v, w=w, u=u: v.tensor_copy(
                        stg[u % 2][0:w, 0:128], ps_t[u % 2][0:w, 0:128]), deps=[ttr])
                    if kind == "PxVh":
                        if jt < 8:
                            tgt = ds["PxT_s"].ap()[jt, :, g * HC + c * 128: g * HC + (c + 1) * 128]
                        else:
                            tgt = ds["VhT_s"].ap()[jt - 8, :, g * HC + c * 128: g * HC + (c + 1) * 128]
                    else:
                        tgt = ds["outwT_s"].ap()[jt, :, c * 128:(c + 1) * 128]
                    tevac[u] = P.dma("a", tgt, stg[u % 2][0:w, 0:128], deps=[te])
                u += 1
    P.barrier()

    def ln_chain(P, xin, cen, sc2, stt, deps):
        tr1 = P.run("v", lambda v: v.reduce_sum(stt[:, 0:1], xin[:], AX.X), deps=deps)
        tr2 = P.run("v", lambda v: v.tensor_scalar_mul(stt[:, 1:2], stt[:, 0:1], 1.0 / D), deps=[tr1])
        tr3 = P.run("v", lambda v: v.tensor_scalar_sub(cen[:], xin[:], stt[:, 1:2]), deps=[tr2])
        tr4 = P.run("v", lambda v: v.tensor_mul(sc2[:], cen[:], cen[:]), deps=[tr3])
        tr5 = P.run("v", lambda v: v.reduce_sum(stt[:, 2:3], sc2[:], AX.X), deps=[tr4])
        tr5b = P.run("v", lambda v: v.tensor_scalar(stt[:, 3:4], stt[:, 2:3], 1.0 / D, LN_EPS,
                                                    ALU.mult, ALU.add), deps=[tr5])
        tr6 = P.run("a", lambda a: a.activation(stt[:, 3:4], stt[:, 3:4], ACT.Sqrt), deps=[tr5b])
        tr6b = P.run("v", lambda v: v.reciprocal(stt[:, 3:4], stt[:, 3:4]), deps=[tr6])
        tr7 = P.run("v", lambda v: v.tensor_scalar_mul(cen[:], cen[:], stt[:, 3:4]), deps=[tr6b])
        return tr7

    def quant_chain(P, xin, q1, qi, outb, ncol, stt, deps):
        tq0 = P.run("v", lambda v: v.reduce_max(stt[:, 4:5], xin[:, 0:ncol], AX.X,
                                                apply_absolute_value=True), deps=deps)
        tq1 = P.run("v", lambda v: v.tensor_scalar_max(stt[:, 4:5], stt[:, 4:5], 1e-5), deps=[tq0])
        tq2 = P.run("v", lambda v: v.reciprocal(stt[:, 5:6], stt[:, 4:5]), deps=[tq1])
        tq3 = P.run("v", lambda v: v.tensor_scalar_mul(stt[:, 6:7], stt[:, 5:6], 127.0), deps=[tq2])
        tq4 = P.run("v", lambda v: v.tensor_scalar_mul(stt[:, 7:8], stt[:, 4:5], 1.0 / 127.0), deps=[tq2])
        tqb = P.run("v", lambda v: [v.tensor_scalar_mul(stt[:, 8:9], stt[:, 7:8], -16384.0),
                                    v.drain(),
                                    v.memset(stt[:, 9:10], 16384.0)][-1], deps=[tq4])
        ty = P.run("a", lambda a: a.activation(q1[:, 0:ncol], xin[:, 0:ncol], ACT.Identity,
                                               bias=stt[:, 9:10], scale=stt[:, 6:7]),
                   deps=[tq3, tqb])
        tc1 = P.run("v", lambda v: v.tensor_copy(qi[:, 0:ncol], q1[:, 0:ncol]), deps=[ty])
        td = P.run("v", lambda v: v.tensor_sub(q1[:, 0:ncol], q1[:, 0:ncol], qi[:, 0:ncol]), deps=[tc1])
        tg = P.run("v", lambda v: v.tensor_single_scalar(q1[:, 0:ncol], q1[:, 0:ncol], 0.5, ALU.is_ge),
                   deps=[td])
        tr2 = P.run("v", lambda v: v.tensor_add(q1[:, 0:ncol], q1[:, 0:ncol], qi[:, 0:ncol]), deps=[tg])
        to = P.run("a", lambda a: a.activation(outb[:, 0:ncol], q1[:, 0:ncol], ACT.Identity,
                                               bias=stt[:, 8:9], scale=stt[:, 7:8]),
                   deps=[tr2, tqb])
        return to

    for s in range(SEQ):
        with ExitStack() as px:
            XS = lambda n, sh, d=F32: px.enter_context(nc.sbuf_tensor(U(n), sh, d))
            xt = [XS(f"a_x{i}", [128, D]) for i in range(2)]
            cen = [XS(f"a_cen{i}", [128, D]) for i in range(2)]
            sc2 = XS("a_sc2", [128, D])
            stt = XS("a_stt", [128, 8])
            xnb = [XS(f"a_xnb{i}", [128, D], BF16) for i in range(2)]
            xnT = XS("a_xnT", [128, 8 * T], BF16)
            wq_sb = XS("a_wq", [128, 8 * D3], BF16)
            iacc = XS("a_iacc", [128, 8])
            imax = XS("a_imax", [128, 1])
            qkst = [XS(f"a_qkst{i}", [128, T], BF16) for i in range(2)]
            vst = [XS(f"a_vst{i}", [128, NH * (HD + 1)], BF16) for i in range(2)]
            ps_tr = [px.enter_context(nc.psum_tensor(U(f"a_ptr{i}"), [128, 128], BF16)) for i in range(2)]
            HW2 = min(1024, T)
            ps_qk = [px.enter_context(nc.psum_tensor(U(f"a_pqk{i}"), [128, max(HW2, D)], F32)) for i in range(2)]
            ps_im = px.enter_context(nc.psum_tensor(U("a_pim"), [1, 128], F32))

            twq = [P.dma("s", wq_sb[:, kc * D3:(kc + 1) * D3], ds["wqkv_b"].ap()[kc]) for kc in range(8)]
            t_tr = {}
            t_ln = {}
            for tt in range(TT):
                par = tt % 2
                tl = P.dma("s", xt[par][:], din["L0_attn_x"].ap()[s, tt * 128:(tt + 1) * 128, :],
                           deps=[t_ln.get(tt - 2)])
                tln = ln_chain(P, xt[par], cen[par], sc2, stt,
                               [tl, t_ln.get(tt - 2)])
                t_ln[tt] = tcast = P.run("v", lambda v, par=par: v.tensor_copy(xnb[par][:], cen[par][:]),
                                         deps=[tln])
                for j in range(8):
                    u = tt * 8 + j
                    ttr = P.run("t", lambda t, par=par, j=j, u=u: t.transpose(
                        ps_tr[u % 2][:], xnb[par][:, j * 128:(j + 1) * 128], eye_b[:]),
                        deps=[tcast, t_tr.get(u - 2)])
                    t_tr[u] = P.run("v", lambda v, tt=tt, j=j, u=u: v.tensor_copy(
                        xnT[:, j * T + tt * 128: j * T + (tt + 1) * 128], ps_tr[u % 2][:]), deps=[ttr])
            t_xnT = [t_tr[u] for u in range(8 * TT)]
            tia = [P.run("v", lambda v, j=j: v.reduce_sum(iacc[:, j:j + 1], xnT[:, j * T:(j + 1) * T],
                                                          AX.X, apply_absolute_value=True), deps=t_xnT)
                   for j in range(8)]
            tim = P.run("v", lambda v: v.reduce_max(imax[:], iacc[:], AX.X), deps=tia)
            tit = P.run("t", lambda t: t.transpose(ps_im[:], imax[:, 0:1], eye_f[:]), deps=[tim])
            P.run("v", lambda v: v.reduce_max(inf_sb[:, s:s + 1], ps_im[0:1, :], AX.X), deps=[tit])

            t_ev = {}
            NHF = T // HW2
            u = 0
            for mt in range(16):
                tevs = []
                for hf in range(NHF):
                    par = u % 2

                    def qkmm(t, mt=mt, hf=hf, par=par):
                        for kc in range(8):
                            for n0, nn in nsplit(HW2, 512):
                                mm = t.matmul(ps_qk[par][:, n0:n0 + nn],
                                              wq_sb[:, kc * D3 + mt * 128: kc * D3 + (mt + 1) * 128],
                                              xnT[:, kc * T + hf * HW2 + n0: kc * T + hf * HW2 + n0 + nn],
                                              start=(kc == 0), stop=(kc == 7))
                        return mm
                    tmm = P.run("t", qkmm, deps=t_xnT + twq + [t_ev.get(u - 2)])
                    tev = P.run("v", lambda v, mt=mt, hf=hf, par=par: v.tensor_scalar_add(
                        qkst[mt % 2][:, hf * HW2:(hf + 1) * HW2], ps_qk[par][:, 0:HW2],
                        bqkv_sb[:, mt:mt + 1]), deps=[tmm])
                    tevs.append(tev)
                    t_ev[u] = tev
                    u += 1
                tgt = ds["q_s"] if mt < 8 else ds["k_s"]
                t_ev[u - 1] = P.dma("a", tgt.ap()[s, mt % 8], qkst[mt % 2][:], deps=tevs)
            tz = P.run("v", lambda v: [v.memset(vst[i][:, h * (HD + 1) + HD: h * (HD + 1) + HD + 1], 1.0)
                                       for i in range(2) for h in range(NH)][-1])
            t_vev = {}
            for tt in range(TT):
                par = tt % 2

                def vmm(t, tt=tt, par=par):
                    for kc in range(8):
                        for n0, nn in nsplit(D, 512):
                            mm = t.matmul(ps_qk[par][:, n0:n0 + nn],
                                          xnT[:, kc * T + tt * 128: kc * T + (tt + 1) * 128],
                                          wq_sb[:, kc * D3 + 2 * D + n0: kc * D3 + 2 * D + n0 + nn],
                                          start=(kc == 0), stop=(kc == 7))
                    return mm
                tmm = P.run("t", vmm, deps=[t_ev.get(16 * NHF - 2), t_ev.get(16 * NHF - 1),
                                            t_vev.get(tt - 2)])

                def vevac(v, tt=tt, par=par):
                    for h in range(NH):
                        mm = v.tensor_add(vst[par][:, h * (HD + 1): h * (HD + 1) + HD],
                                          ps_qk[par][:, h * HD:(h + 1) * HD],
                                          bvbc[:, h * HD:(h + 1) * HD])
                    return mm
                tev = P.run("v", vevac, deps=[tmm, tz, t_vev.get(tt - 2)])
                t_vev[tt] = P.dma("s", ds["v_s"].ap()[s, tt], vst[par][:], deps=[tev])
        P.barrier()

        with ExitStack() as px:
            XS = lambda n, sh, d=F32: px.enter_context(nc.sbuf_tensor(U(n), sh, d))
            q_ct = XS("at_q", [128, T], BF16)
            k_ct = XS("at_k", [128, T], BF16)
            scr = XS("at_scr", [128, T], BF16)
            red1 = XS("at_red", [64, 1])
            tt8 = XS("at_t8", [1, 8])
            splat = XS("at_spl", [128, 2 * NH])
            with ExitStack() as px2:
                ps_st = px2.enter_context(nc.psum_tensor(U("at_pst"), [64, T], F32))
                ps_tp = px2.enter_context(nc.psum_tensor(U("at_ptp"), [1, 64], F32))
                ps_sp = px2.enter_context(nc.psum_tensor(U("at_psp"), [128, 2], F32))
                tspl = {}
                prev_c1 = None
                prev_c2 = None
                prev_v1 = None
                prev_v2 = None
                prev_v4 = None
                for ct in range(8):
                    tq = P.dma("s", q_ct[:], ds["q_s"].ap()[s, ct], deps=[tspl.get(2 * ct - 1)])
                    tk = P.dma("s", k_ct[:], ds["k_s"].ap()[s, ct], deps=[tspl.get(2 * ct - 1)])
                    for hh in range(2):
                        h = ct * 2 + hh
                        p0 = 64 * hh
                        ta1 = P.run("a", lambda a, p0=p0: a.activation(scr[0:64, :], q_ct[p0:p0 + 64, :],
                                                                       ACT.Abs), deps=[tq, prev_c2])

                        def csum(t, row):
                            r0 = row * 32
                            for n0, nn in nsplit(T, 512):
                                mm = t.matmul(ps_st[r0:r0 + 1, n0:n0 + nn], ones64[:],
                                              scr[0:64, n0:n0 + nn], start=True, stop=True)
                            return mm
                        tc1 = P.run("t", lambda t: csum(t, 0), deps=[ta1, prev_v1])
                        tv1 = P.run("v", lambda v: v.reduce_max(tt8[:, 0:1], ps_st[0:1, :], AX.X),
                                    deps=[tc1])
                        ta2 = P.run("a", lambda a, p0=p0: a.activation(scr[0:64, :], q_ct[p0:p0 + 64, :],
                                                                       ACT.Square), deps=[tc1])
                        tc2 = P.run("t", lambda t: csum(t, 1), deps=[ta2, prev_v2])
                        prev_c1, prev_c2 = tc1, tc2
                        prev_v1, prev_v2 = tv1, None
                        tv2 = P.run("v", lambda v: v.reduce_sum(tt8[:, 1:2], ps_st[32:33, :], AX.X),
                                    deps=[tc2])
                        prev_v2 = tv2
                        tv3 = P.run("v", lambda v, p0=p0: v.reduce_sum(red1[:], k_ct[p0:p0 + 64, :], AX.X,
                                                                       apply_absolute_value=True),
                                    deps=[tk, prev_v4]) if True else None
                        tt3 = P.run("t", lambda t: t.transpose(ps_tp[:], red1[:, 0:1],
                                                               eye_f[0:64, 0:64]),
                                    deps=[tv3, prev_v4])
                        tv4 = P.run("v", lambda v: v.reduce_max(tt8[:, 2:3], ps_tp[0:1, :], AX.X),
                                    deps=[tt3])
                        prev_v4 = tv4
                        ta3 = P.run("a", lambda a: a.activation(tt8[:, 3:4], tt8[:, 1:2], ACT.Sqrt),
                                    deps=[tv2])
                        tv5 = P.run("v", lambda v: v.tensor_mul(tt8[:, 4:5], tt8[:, 3:4],
                                                                inf_sb[:, s:s + 1]), deps=[ta3])
                        tv6 = P.run("v", lambda v: v.reciprocal(tt8[:, 5:6], tt8[:, 4:5]), deps=[tv5])
                        tv7 = P.run("v", lambda v: v.tensor_scalar_mul(tt8[:, 6:7], tt8[:, 5:6], 2.0),
                                    deps=[tv6])
                        tv8 = P.run("v", lambda v: v.tensor_mul(tt8[:, 7:8], tt8[:, 0:1], tt8[:, 2:3]),
                                    deps=[tv1, tv4])
                        tv9 = P.run("v", lambda v: v.scalar_tensor_tensor(tt8[:, 7:8], tt8[:, 7:8], -2.0,
                                                                          tt8[:, 6:7], ALU.mult, ALU.mult),
                                    deps=[tv8, tv7])
                        tsp = P.run("t", lambda t: t.matmul(ps_sp[:], ones1[:], tt8[:, 6:8],
                                                            start=True, stop=True), deps=[tv9])
                        tspl[h] = P.run("v", lambda v, h=h: v.tensor_copy(splat[:, 2 * h:2 * h + 2], ps_sp[:]),
                                        deps=[tsp])
            v_full = XS("at_vf", [128, TT * NH * (HD + 1)], BF16)
            QW = min(1024, T)
            NU = T // QW
            attnT = [XS(f"at_aT{i}", [128, QW], BF16) for i in range(4)]
            qb2 = [XS(f"at_q2{i}", [128, T], BF16) for i in range(2)]
            kb2 = [XS(f"at_k2{i}", [128, T], BF16) for i in range(2)]
            av_f = XS("at_avf", [64, T])
            rz = XS("at_rz", [1, T])
            attn_fm = XS("at_fm", [128, 8 * T], BF16)
            ps_sc = px.enter_context(nc.psum_tensor(U("at_psc"), [128, T], F32))
            ps_av = px.enter_context(nc.psum_tensor(U("at_pav"), [128, T], F32))
            tvf = [P.dma("s", v_full[:, tt * NH * (HD + 1):(tt + 1) * NH * (HD + 1)],
                         ds["v_s"].ap()[s, tt]) for tt in range(TT)]
            t_fm = {}
            exp_tail = [None] * max(NU, 2)
            tlast_prev = None
            NUU = TT * NU
            for ct in range(8):
                qcb, kcb = qb2[ct % 2], kb2[ct % 2]
                tq = P.dma("s", qcb[:], ds["q_s"].ap()[s, ct],
                           deps=[t_fm.get(2 * ct - 3), tspl[15]])
                tk = P.dma("s", kcb[:], ds["k_s"].ap()[s, ct], deps=[t_fm.get(2 * ct - 3)])
                for hh in range(2):
                    h = ct * 2 + hh
                    p0 = 64 * hh
                    texp = {}
                    tavd = {}

                    def emit_av(w, kchunk, qh, h=h):
                        def avmm(t, kchunk=kchunk, qh=qh, h=h, w=w):
                            vof = kchunk * NH * (HD + 1) + h * (HD + 1)
                            for n0, nn in nsplit(QW, 512):
                                mm = t.matmul(ps_av[0:HD + 1, qh * QW + n0: qh * QW + n0 + nn],
                                              v_full[:, vof:vof + HD + 1],
                                              attnT[w % 4][:, n0:n0 + nn],
                                              start=(kchunk == 0), stop=(kchunk == TT - 1))
                            return mm
                        tavd[w] = P.run("t", avmm, deps=[texp[w]] + tvf +
                                        ([t_fm.get(h - 1)] if kchunk == 0 else []))

                    av_args = {}
                    for kchunk in range(TT):
                        for qh in range(NU):
                            u = kchunk * NU + qh

                            def smm(t, kchunk=kchunk, qh=qh, p0=p0, qcb=qcb, kcb=kcb):
                                for n0, nn in nsplit(QW, 512):
                                    mm = t.matmul(ps_sc[:, qh * QW + n0: qh * QW + n0 + nn],
                                                  kcb[p0:p0 + 64, kchunk * 128:(kchunk + 1) * 128],
                                                  qcb[p0:p0 + 64, qh * QW + n0: qh * QW + n0 + nn],
                                                  start=True, stop=True)
                                return mm
                            tmm = P.run("t", smm,
                                        deps=[tq, tk,
                                              texp.get(u - NU, exp_tail[u] if u < NU else None),
                                              tspl[h], t_fm.get(h - 1)])
                            texp[u] = P.run("a", lambda a, qh=qh, h=h, u=u: a.activation(
                                attnT[u % 4][:], ps_sc[:, qh * QW:(qh + 1) * QW], ACT.Exp,
                                bias=splat[:, 2 * h + 1:2 * h + 2], scale=splat[:, 2 * h:2 * h + 1]),
                                deps=[tmm, tavd.get(u - 4, tlast_prev if u < 4 else None)])
                            av_args[u] = (kchunk, qh)
                            if u >= 1:
                                emit_av(u - 1, *av_args[u - 1])
                    emit_av(NUU - 1, *av_args[NUU - 1])
                    tlast = tavd[NUU - 1]
                    trz = P.run("v", lambda v: v.reciprocal(rz[:], ps_av[HD:HD + 1, :]), deps=[tlast])
                    tae = P.run("v", lambda v: v.tensor_copy(av_f[:], ps_av[0:HD, :]), deps=[tlast])

                    def bcast(t):
                        for n0, nn in nsplit(T, 512):
                            mm = t.matmul(ps_av[:, n0:n0 + nn], ones1[:], rz[:, n0:n0 + nn],
                                          start=True, stop=True)
                        return mm
                    tbc = P.run("t", bcast, deps=[trz, tae])
                    t_fm[h] = P.run("v", lambda v, h=h, p0=p0: v.tensor_mul(
                        attn_fm[p0:p0 + 64, (h // 2) * T:(h // 2 + 1) * T], av_f[:], ps_av[0:64, :]),
                        deps=[tbc, tae])
                    exp_tail = [texp[NUU - NU + i] for i in range(NU)]
                    tlast_prev = tlast
            wp_sb = XS("at_wp", [128, 8 * D], BF16)
            s0st = [XS(f"at_s0{i}", [128, T]) for i in range(2)]
            twp = [P.dma("s", wp_sb[:, kc * D:(kc + 1) * D], ds["wproj_b"].ap()[kc]) for kc in range(8)]
            tfm = [t_fm[h] for h in range(NH)]
            tpj = {}
            for mt in range(8):
                par = mt % 2
                pp = ps_av if par == 0 else ps_sc

                def pjmm(t, mt=mt, pp=pp):
                    for kc in range(8):
                        for n0, nn in nsplit(T, 512):
                            mm = t.matmul(pp[:, n0:n0 + nn],
                                          wp_sb[:, kc * D + mt * 128: kc * D + (mt + 1) * 128],
                                          attn_fm[:, kc * T + n0: kc * T + n0 + nn],
                                          start=(kc == 0), stop=(kc == 7))
                    return mm
                tmm = P.run("t", pjmm, deps=tfm + twp + [tpj.get(mt - 2)])
                tev = P.run("v", lambda v, mt=mt, par=par, pp=pp: v.tensor_scalar_add(
                    s0st[par][:], pp[:], bproj_sb[:, mt:mt + 1]), deps=[tmm])
                tpj[mt] = P.dma("a", ds["s0_s"].ap()[s, mt], s0st[par][:], deps=[tev])
        P.barrier()

        CH = min(256, T)
        TPC = CH // 128
        NCH = T // CH
        with ExitStack() as px:
            XS = lambda n, sh, d=F32: px.enter_context(nc.sbuf_tensor(U(n), sh, d))
            xcl = [[XS(f"b_xc{p}{i}", [128, D]) for i in range(TPC)] for p in range(2)]
            cen = XS("b_cen", [128, D])
            sc2 = XS("b_sc2", [128, D])
            stt = XS("b_stt", [128, 12])
            sc2b = XS("b_sc2b", [128, D])
            sttb = XS("b_sttb", [128, 12])
            cenb = XS("b_cenb", [128, D])
            q1 = XS("b_q1", [128, INNER])
            qi = XS("b_qi", [128, INNER], mybir.dt.int32)
            xqb = [XS(f"b_xqb{p}", [128, D], BF16) for p in range(2)]
            xq1T = [XS(f"b_xq1T{p}", [128, 8 * CH], BF16) for p in range(2)]
            ws_g = XS("b_wsg", [128, 8 * 1024], BF16)
            ws_u = XS("b_wsu", [128, 8 * 1024], BF16)
            wsd = [XS(f"b_wsd{i}", [128, D], BF16) for i in range(2)]
            t2sb = XS("b_t2", [128, 32 * CH], BF16)
            t2T = [XS(f"b_t2T{i}", [128, INNER], BF16) for i in range(2)]
            xq2T = XS("b_xq2T", [128, 32 * CH], BF16)
            gbuf = XS("b_gb", [128, 2 * CH])
            g0 = XS("b_g0", [128, 8 * CH])
            s0sl = XS("b_s0", [128, CH])
            rml = XS("b_rm", [128, CH])
            xn1f = XS("b_xn1f", [128, D])
            xnb = XS("b_xnb", [128, D], BF16)
            stgT = XS("b_stgT", [128, 8 * CH], BF16)
            stgT2 = XS("b_stgT2", [128, 8 * CH], BF16)

            t_t2 = {}
            tdmm = {}
            tg0_d = {}
            q1st = {}
            q2st = {}
            gu_l = {}
            cb_l = {}
            trm_l = {}
            LQ = [None]
            PF = [[]]

            def stage_q1(cc):
                p = cc % 2
                res = {"tq1T": []}
                with ExitStack() as pxa:
                    ps_trA = [pxa.enter_context(nc.psum_tensor(U(f"b_pq1_{i}"), [128, 128], BF16))
                              for i in range(2)]
                    trev = {}
                    lt = None
                    res["lnend"] = {}
                    for tto in range(TPC):
                        tt = cc * TPC + tto
                        tx = P.dma("s", xcl[p][tto][:],
                                   din["L0_attn_x"].ap()[s, tt * 128:(tt + 1) * 128, :],
                                   deps=[LQ[0]] + cb_l.get(cc - 2, []))
                        tln = ln_chain(P, xcl[p][tto], cen, sc2, stt, [tx, LQ[0]])
                        res["lnend"][tto] = tln
                        tqd = quant_chain(P, cen, q1, qi, xqb[p], D, stt,
                                          [tln, q1st.get(cc - 2, {}).get("trl")])
                        LQ[0] = tqd
                        for j in range(8):
                            u = tto * 8 + j
                            ttr = P.run("t", lambda t, ps=ps_trA, p=p, j=j, u=u: t.transpose(
                                ps[u % 2][:], xqb[p][:, j * 128:(j + 1) * 128], eye_b[:]),
                                deps=[tqd, trev.get(u - 2)] + (PF[0] if u < 2 else []))
                            trev[u] = P.run("v", lambda v, ps=ps_trA, p=p, j=j, tto=tto, u=u: v.tensor_copy(
                                xq1T[p][:, j * CH + tto * 128: j * CH + (tto + 1) * 128], ps[u % 2][:]),
                                deps=[ttr, gu_l.get(cc - 2)])
                            res["tq1T"].append(trev[u])
                            lt = ttr
                    res["trl"] = lt
                PF[0] = [res["tq1T"][-1]]
                q1st[cc] = res

            def stage_gu(cc):
                p = cc % 2
                tq1T = q1st[cc]["tq1T"]
                with ExitStack() as pxa:
                    ps_g = [pxa.enter_context(nc.psum_tensor(U(f"b_pg{i}"), [128, CH], F32)) for i in range(2)]
                    ps_u = [pxa.enter_context(nc.psum_tensor(U(f"b_pu{i}"), [128, CH], F32)) for i in range(2)]
                    te1 = None
                    for og in range(4):
                        twg = [P.dma("s", ws_g[:, kc * 1024:(kc + 1) * 1024],
                                     ds["wqT_g"].ap()[kc, :, og * 1024:(og + 1) * 1024],
                                     deps=[t_t2.get(cc * 32 + (og - 1) * 8 + 7)]) for kc in range(8)]
                        twu = [P.dma("s", ws_u[:, kc * 1024:(kc + 1) * 1024],
                                     ds["wqT_u"].ap()[kc, :, og * 1024:(og + 1) * 1024],
                                     deps=[t_t2.get(cc * 32 + (og - 1) * 8 + 7)]) for kc in range(8)]
                        for m8 in range(8):
                            mt = og * 8 + m8
                            fk = cc * 32 + mt
                            par = mt % 2

                            def gmm(t, which, m8=m8, par=par, ps_g=ps_g, ps_u=ps_u, p=p):
                                ws = ws_g if which == 0 else ws_u
                                pp = ps_g[par] if which == 0 else ps_u[par]
                                for kc in range(8):
                                    mm = t.matmul(pp[:, 0:CH],
                                                  ws[:, kc * 1024 + m8 * 128: kc * 1024 + (m8 + 1) * 128],
                                                  xq1T[p][:, kc * CH:(kc + 1) * CH],
                                                  start=(kc == 0), stop=(kc == 7))
                                return mm
                            tg = P.run("t", lambda t, g=gmm: g(t, 0),
                                       deps=tq1T + twg + [t_t2.get(fk - 2)] +
                                            (PF[0] if mt == 0 else []))
                            tu = P.run("t", lambda t, g=gmm: g(t, 1),
                                       deps=twu + [t_t2.get(fk - 2)])
                            te1 = P.run("a", lambda a, par=par, ps_g=ps_g: a.activation(
                                gbuf[:, par * CH:(par + 1) * CH], ps_g[par][:, 0:CH], ACT.Silu,
                                scale=rsw_sb[:, 0:1]), deps=[tg, t_t2.get(fk - 2)])
                            t_t2[fk] = P.run("v", lambda v, mt=mt, par=par, ps_u=ps_u: v.tensor_mul(
                                t2sb[:, mt * CH:(mt + 1) * CH], gbuf[:, par * CH:(par + 1) * CH],
                                ps_u[par][:, 0:CH]),
                                deps=[te1, tu, q2st.get(cc - 1, {}).get("tr1l")])
                    gu_l[cc] = tu
                PF[0] = [t_t2[cc * 32 + 31], te1]

            def stage_q2(cc):
                res = {"tq2T": []}
                t2rdy = t_t2[cc * 32 + 31]
                prev = q2st.get(cc - 1, {})
                with ExitStack() as pxa:
                    ps_trB = [pxa.enter_context(nc.psum_tensor(U(f"b_pq2_{i}"), [128, 128], BF16))
                              for i in range(2)]
                    trev = {}
                    u = 0
                    tev_t = {}
                    for tto in range(TPC):
                        tev = []
                        for ot in range(32):
                            ttr = P.run("t", lambda t, ps=ps_trB, ot=ot, tto=tto, u=u: t.transpose(
                                ps[u % 2][:], t2sb[:, ot * CH + tto * 128: ot * CH + (tto + 1) * 128],
                                eye_b[:]),
                                deps=[t2rdy, trev.get(u - 2), prev.get("tr2l")] +
                                     (PF[0] if u < 2 else []))
                            trev[u] = P.run("v", lambda v, ps=ps_trB, ot=ot, tto=tto, u=u: v.tensor_copy(
                                t2T[tto][:, ot * 128:(ot + 1) * 128], ps[u % 2][:]),
                                deps=[ttr, LQ[0], prev.get("tr2l")])
                            tev.append(trev[u])
                            u += 1
                        res["tr1l"] = ttr
                        tev_t[tto] = tev
                    for tto in range(TPC):
                        tqd = quant_chain(P, t2T[tto], q1, qi, t2T[tto], INNER, stt,
                                          tev_t[tto] + [LQ[0]])
                        LQ[0] = tqd
                        for j in range(32):
                            ttr = P.run("t", lambda t, ps=ps_trB, tto=tto, j=j, u=u: t.transpose(
                                ps[u % 2][:], t2T[tto][:, j * 128:(j + 1) * 128], eye_b[:]),
                                deps=[tqd, trev.get(u - 2)])
                            trev[u] = P.run("v", lambda v, ps=ps_trB, j=j, tto=tto, u=u: v.tensor_copy(
                                xq2T[:, j * CH + tto * 128: j * CH + (tto + 1) * 128], ps[u % 2][:]),
                                deps=[ttr, tdmm.get((cc - 1) * 32 + 31)])
                            res["tq2T"].append(trev[u])
                            u += 1
                        res["tr2l"] = ttr
                PF[0] = [res["tq2T"][-1]]
                q2st[cc] = res

            def stage_dn(cc):
                tq2T = q2st[cc]["tq2T"]
                with ExitStack() as pxa:
                    ps8 = [pxa.enter_context(nc.psum_tensor(U(f"b_p8{i}"), [128, CH], F32)) for i in range(8)]
                    for kc in range(32):
                        fk = cc * 32 + kc
                        twd = P.dma("s", wsd[kc % 2][:], ds["wqT_d"].ap()[kc],
                                    deps=[tdmm.get(fk - 2)] + (tq2T if kc < 2 else []))

                        def dmm(t, kc=kc, ps8=ps8):
                            for mt in range(8):
                                mm = t.matmul(ps8[mt][:, 0:CH],
                                              wsd[kc % 2][:, mt * 128:(mt + 1) * 128],
                                              xq2T[:, kc * CH:(kc + 1) * CH],
                                              start=(kc == 0), stop=(kc == 31))
                            return mm
                        tdmm[fk] = P.run("t", dmm, deps=[twd] + tq2T +
                                         ([trm_l.get(cc - 1)] + PF[0] if kc == 0 else []))
                    tg0_d[cc] = [P.run("a", lambda a, mt=mt, ps8=ps8: a.activation(
                        g0[:, mt * CH:(mt + 1) * CH], ps8[mt][:, 0:CH], ACT.Silu,
                        scale=rsw_sb[:, 4:5]), deps=[tdmm[cc * 32 + 31], trm_l.get(cc - 1)])
                        for mt in range(8)]
                PF[0] = [tg0_d[cc][-1]]

            def stage_cb(cc):
                p = cc % 2
                prev_end = cb_l.get(cc - 1, [])
                with ExitStack() as pxa:
                    ps_tr = [pxa.enter_context(nc.psum_tensor(U(f"b_pc{i}"), [128, 128], F32)) for i in range(2)]
                    ps_trb = [pxa.enter_context(nc.psum_tensor(U(f"b_pcb{i}"), [128, 128], BF16)) for i in range(2)]
                    u = 0
                    ub = 0
                    trev = {}
                    trevb = {}
                    tx1 = {tto: [] for tto in range(TPC)}
                    trm = None
                    for mt in range(8):
                        ts0 = P.dma("s", s0sl[:], ds["s0_s"].ap()[s, mt][:, cc * CH:(cc + 1) * CH],
                                    deps=[trm] + (prev_end if mt == 0 else []))
                        trm = P.run("v", lambda v, mt=mt: v.tensor_mul(rml[:], s0sl[:],
                                                                       g0[:, mt * CH:(mt + 1) * CH]),
                                    deps=[ts0, tg0_d[cc][mt]])
                        for tto in range(TPC):
                            ttr = P.run("t", lambda t, ps=ps_tr, tto=tto, u=u: t.transpose(
                                ps[u % 2][:], rml[:, tto * 128:(tto + 1) * 128], eye_f[:]),
                                deps=[trm, trev.get(u - 2)] + (PF[0] if u < 2 else []))
                            trev[u] = P.run("v", lambda v, ps=ps_tr, p=p, mt=mt, tto=tto, u=u: v.tensor_add(
                                xcl[p][tto][:, mt * 128:(mt + 1) * 128], xcl[p][tto][:, mt * 128:(mt + 1) * 128],
                                ps[u % 2][:]), deps=[ttr, q1st[cc]["lnend"][tto]] +
                                (prev_end if u == 0 else []))
                            tx1[tto].append(trev[u])
                            u += 1
                    trm_l[cc] = trm
                    t_end = []
                    lnp = None
                    for tto in range(TPC):
                        tt = cc * TPC + tto
                        tst = P.dma("s", ds["x1_s"].ap()[s, tt * 128:(tt + 1) * 128, :], xcl[p][tto][:],
                                    deps=tx1[tto])
                        tln = ln_chain(P, xcl[p][tto], xn1f, sc2b, sttb, tx1[tto] + [lnp])
                        tcast = P.run("v", lambda v: v.tensor_copy(xnb[:], xn1f[:]), deps=[tln])
                        for j in range(8):
                            ttr = P.run("t", lambda t, j=j, ub=ub: t.transpose(
                                ps_trb[ub % 2][:], xnb[:, j * 128:(j + 1) * 128], eye_b[:]),
                                deps=[tcast, trevb.get(ub - 2)])
                            trevb[ub] = P.run("a", lambda a, j=j, tto=tto, ub=ub: a.activation(
                                stgT[:, j * CH + tto * 128: j * CH + (tto + 1) * 128],
                                ps_trb[ub % 2][:], ACT.Identity), deps=[ttr])
                            t_end.append(trevb[ub])
                            ub += 1
                        tln2 = ln_chain(P, xn1f, cenb, sc2b, sttb, [tcast])
                        lnp = tln2
                        tcast2 = P.run("v", lambda v: v.tensor_copy(xnb[:], cenb[:]),
                                       deps=[tln2, ttr])
                        for j in range(8):
                            ttr = P.run("t", lambda t, j=j, ub=ub: t.transpose(
                                ps_trb[ub % 2][:], xnb[:, j * 128:(j + 1) * 128], eye_b[:]),
                                deps=[tcast2, trevb.get(ub - 2)])
                            trevb[ub] = P.run("a", lambda a, j=j, tto=tto, ub=ub: a.activation(
                                stgT2[:, j * CH + tto * 128: j * CH + (tto + 1) * 128],
                                ps_trb[ub % 2][:], ACT.Identity,
                                bias=bet_sb[:, j:j + 1], scale=gam_sb[:, j:j + 1]), deps=[ttr])
                            t_end.append(trevb[ub])
                            ub += 1
                        t_end.append(tst)
                    tsp1 = [P.dma("s", ds["xn1T_s"].ap()[s, j, :, cc * CH:(cc + 1) * CH],
                                  stgT[:, j * CH:(j + 1) * CH], deps=t_end) for j in range(8)]
                    tsp2 = [P.dma("s", ds["xniT_s"].ap()[s, j, :, cc * CH:(cc + 1) * CH],
                                  stgT2[:, j * CH:(j + 1) * CH], deps=t_end) for j in range(8)]
                    cb_l[cc] = tsp1 + tsp2
                PF[0] = [trev[u - 1], trevb[ub - 1]]

            stage_q1(0)
            stage_gu(0)
            if NCH > 1:
                stage_q1(1)
            for cc in range(NCH):
                stage_q2(cc)
                stage_dn(cc)
                if cc + 1 < NCH:
                    stage_gu(cc + 1)
                stage_cb(cc)
                if cc + 2 < NCH:
                    stage_q1(cc + 2)
        P.barrier()

        with ExitStack() as px:
            XS = lambda n, sh, d=F32: px.enter_context(nc.sbuf_tensor(U(n), sh, d))
            xniT = XS("c_xniT", [128, 8 * T], BF16)
            ballT = XS("c_ball", [128, T])
            apw = XS("c_apw", [128, T])
            asf = XS("c_asf", [128, T])
            dT = XS("c_dT", [128, T])
            cum = XS("c_cum", [128, T])
            stb = XS("c_stb", [128, T], BF16)
            ssm = XS("c_ssm", [128, 8 * T], BF16)
            pxT = XS("c_pxT", [128, 8 * G3], BF16)
            pst = [XS(f"c_pst{i}", [128, T]) for i in range(2)]
            onesT = XS("c_onesT", [128, T])
            ps1 = px.enter_context(nc.psum_tensor(U("c_ps1"), [128, T], F32))
            ps2 = px.enter_context(nc.psum_tensor(U("c_ps2"), [128, T], F32))

            txn = [P.dma("s", xniT[:, j * T:(j + 1) * T], ds["xniT_s"].ap()[s, j]) for j in range(8)]
            tap = P.dma("s", apw[:], ds["apow_d"].ap())
            tas = P.dma("s", asf[:], ds["asafe_d"].ap())

            def bmm(t):
                for kc in range(8):
                    for n0, nn in nsplit(T, 512):
                        mm = t.matmul(ps1[:, n0:n0 + nn], BwT_sb[:, kc * 128:(kc + 1) * 128],
                                      xniT[:, kc * T + n0: kc * T + n0 + nn],
                                      start=(kc == 0), stop=(kc == 7))
                return mm
            tb1 = P.run("t", bmm, deps=txn)
            tb2 = P.run("v", lambda v: v.tensor_scalar_add(ballT[:], ps1[:], Bb_sb[:, 0:1]),
                        deps=[tb1])
            t_on = P.run("v", lambda v: v.memset(onesT[:], 1.0))
            tdr = P.run("v", lambda v: v.reciprocal(dT[:], asf[:]), deps=[tas])
            td = P.run("v", lambda v: v.tensor_mul(dT[:], ballT[:], dT[:]), deps=[tb2, tdr])
            tc = td
            for c0, cn in nsplit(T, 512):
                init = 0.0 if c0 == 0 else cum[:, c0 - 1:c0]
                tc = P.run("v", lambda v, c0=c0, cn=cn, init=init: v.tensor_tensor_scan(
                    cum[:, c0:c0 + cn], onesT[:, c0:c0 + cn],
                    dT[:, c0:c0 + cn], init, ALU.mult, ALU.add), deps=[tc, t_on])
            tst = P.run("v", lambda v: v.tensor_mul(stb[:], apw[:], cum[:]), deps=[tc, tap])
            tssm = []
            for mt in range(8):
                def cmm(t, mt=mt):
                    for n0, nn in nsplit(T, 512):
                        mm = t.matmul(ps2[:, n0:n0 + nn], CwT_sb[:, mt * 128:(mt + 1) * 128],
                                      stb[:, n0:n0 + nn], start=True, stop=True)
                    return mm
                tm2 = P.run("t", cmm, deps=[tst, tssm[-1] if tssm else None])
                tssm.append(P.run("v", lambda v, mt=mt: v.tensor_scalar_add(
                    ssm[:, mt * T:(mt + 1) * T], ps2[:], Cb_sb[:, mt:mt + 1]),
                    deps=[tm2]))
            tpx = [P.dma("s", pxT[:, kc * G3:(kc + 1) * G3], ds["PxT_s"].ap()[kc]) for kc in range(8)]
            tpj = {}
            for mt in range(NM):
                par = mt % 2
                pp = ps1 if par == 0 else ps2

                def pmm(t, mt=mt, pp=pp):
                    for kc in range(8):
                        for n0, nn in nsplit(T, 512):
                            mm = t.matmul(pp[:, n0:n0 + nn],
                                          pxT[:, kc * G3 + mt * 128: kc * G3 + (mt + 1) * 128],
                                          ssm[:, kc * T + n0: kc * T + n0 + nn],
                                          start=(kc == 0), stop=(kc == 7))
                    return mm
                tmm = P.run("t", pmm, deps=tssm + tpx + [tpj.get(mt - 2), tb2 if mt == 0 else None,
                                                         tssm[-1] if mt == 1 else None])
                tev = P.run("v", lambda v, mt=mt, par=par, pp=pp: v.tensor_scalar_add(
                    pst[par][:], pp[:], Pb_sb[:, mt:mt + 1]), deps=[tmm])
                tpj[mt] = P.dma("a", ds["P_s"].ap()[:, mt, s, :], pst[par][:], deps=[tev])
        P.barrier()

    with ExitStack() as px:
        XS = lambda n, sh, d=F32: px.enter_context(nc.sbuf_tensor(U(n), sh, d))
        vh = XS("d_vh", [128, 4 * G3], BF16)
        cfc = XS("d_cfc", [128, T * S4], BF16)
        pbuf = [XS(f"d_pb{i}", [128, NM * SEQ * 128]) for i in range(2)]
        sg = XS("d_sg", [128, NM * SEQ])
        dd = XS("d_dd", [128, 4 * SEQ])
        tm2 = XS("d_tm", [128, 4 * SEQ])
        pp2 = [px.enter_context(nc.psum_tensor(U(f"d_pp{i}"), [128, NM * SEQ], F32)) for i in range(2)]

        tvh = [P.dma("s", vh[:, kc * G3:(kc + 1) * G3], ds["VhT_s"].ap()[kc]) for kc in range(4)]
        NCHK = T // 128
        tpd = {}
        t_t = {}
        tpd[0] = P.dma("s", pbuf[0][:].rearrange("p (m b t) -> p m b t", m=NM, b=SEQ),
                       ds["P_s"].ap()[:, :, :, 0:128])
        if NCHK > 1:
            tpd[1] = P.dma("s", pbuf[1][:].rearrange("p (m b t) -> p m b t", m=NM, b=SEQ),
                           ds["P_s"].ap()[:, :, :, 128:256])
        t_v = {}
        t_a = {}
        t_z = {}

        def pcopy(st):
            pq = pp2[st % 2]
            prhs = pbuf[(st // 128) % 2][:].rearrange("p (m b t) -> p m b t",
                                                      m=NM, b=SEQ)[:, :, :, st % 128]
            return lambda v, pq=pq, prhs=prhs: v.tensor_copy(
                pq[:].rearrange("p (m b) -> p m b", m=NM), prhs)
        t_z[0] = P.run("v", pcopy(0), deps=[tpd[0]])
        t_z[1] = P.run("v", pcopy(1), deps=[tpd[0]])
        for st in range(T):
            chunk = st // 128
            pp = pp2[st % 2]

            def step_mm(t, st=st, pp=pp):
                for m in range(NM):
                    for kc in range(4):
                        if st == 0:
                            rhs = h0_sb[:, kc * SEQ:(kc + 1) * SEQ]
                        else:
                            rhs = cfc[:, (st - 1) * S4 + kc * SEQ: (st - 1) * S4 + (kc + 1) * SEQ]
                        mm = t.matmul(pp[:, m * SEQ:(m + 1) * SEQ],
                                      vh[:, kc * G3 + m * 128: kc * G3 + (m + 1) * 128],
                                      rhs, start=False, stop=False,
                                      skip_group_check=True)
                return mm
            t_t[st] = P.run("t", step_mm,
                            deps=[t_v.get(st - 1), t_z[st]] + (tvh if st == 0 else []))
            t_a[st] = P.run("a", lambda a, pp=pp: a.activation(sg[:], pp[:], ACT.Tanh),
                            deps=[t_t[st]])

            def step_comb(v, st=st):
                v.tensor_sub(dd[:], sg[:, 4 * SEQ:8 * SEQ], sg[:, 0:4 * SEQ])
                v.drain()
                v.scalar_tensor_tensor(tm2[:], sg[:, 8 * SEQ:12 * SEQ], 1.0, dd[:],
                                       ALU.add, ALU.mult)
                v.drain()
                return v.scalar_tensor_tensor(cfc[:, st * S4:(st + 1) * S4], tm2[:], 0.5,
                                              sg[:, 0:4 * SEQ], ALU.mult, ALU.add)
            t_v[st] = P.run("v", step_comb, deps=[t_a[st]])
            if st + 2 < T:
                t_z[st + 2] = P.run("v", pcopy(st + 2),
                                    deps=[t_a[st], tpd[(st + 2) // 128]])
            if st % 128 == 127 and chunk + 2 < NCHK:
                tpd[chunk + 2] = P.dma("s",
                                       pbuf[chunk % 2][:].rearrange("p (m b t) -> p m b t", m=NM, b=SEQ),
                                       ds["P_s"].ap()[:, :, :, (chunk + 2) * 128:(chunk + 3) * 128],
                                       deps=[t_z[st]])
        tcs = P.dma("s", ds["cfc_s"].ap(), cfc[:], deps=[t_v[T - 1]])
    P.barrier()

    for s in range(SEQ):
        with ExitStack() as px:
            XS = lambda n, sh, d=F32: px.enter_context(nc.sbuf_tensor(U(n), sh, d))
            cfc = XS("e_cfc", [128, T * S4], BF16)
            owT = XS("e_ow", [128, 4 * D], BF16)
            ust = XS("e_ust", [128, T])
            stg = [XS(f"e_stg{i}", [128, 128]) for i in range(2)]
            tcf = P.dma("s", cfc[:], ds["cfc_s"].ap())
            tow = [P.dma("s", owT[:, kc * D:(kc + 1) * D], ds["outwT_s"].ap()[kc]) for kc in range(4)]
            cfr = cfc[:].rearrange("p (t g) -> p t g", g=S4)
            with ExitStack() as pxa:
                ps_u = pxa.enter_context(nc.psum_tensor(U("e_psu"), [128, T], F32))
                ps_tr = [pxa.enter_context(nc.psum_tensor(U(f"e_ptr{i}"), [128, 128], F32)) for i in range(2)]
                u = 0
                tpc = {}
                for mt in range(8):
                    def umm(t, mt=mt):
                        for kc in range(4):
                            for n0, nn in nsplit(T, 512):
                                mm = t.matmul(ps_u[:, n0:n0 + nn],
                                              owT[:, kc * D + mt * 128: kc * D + (mt + 1) * 128],
                                              cfr[:, n0:n0 + nn, kc * SEQ + s],
                                              start=(kc == 0), stop=(kc == 3))
                        return mm
                    tmm = P.run("t", umm, deps=[tcf] + tow + [tpc.get(u - 1)])
                    tev = P.run("v", lambda v, mt=mt: v.tensor_scalar_add(ust[:], ps_u[:],
                                                                           outb_sb[:, mt:mt + 1]), deps=[tmm])
                    for tt in range(TT):
                        ttr = P.run("t", lambda t, tt=tt, u=u: t.transpose(
                            ps_tr[u % 2][:], ust[:, tt * 128:(tt + 1) * 128], eye_f[:]),
                            deps=[tev, tpc.get(u - 2)])
                        tec = P.run("v", lambda v, u=u: v.tensor_copy(stg[u % 2][:], ps_tr[u % 2][:]), deps=[ttr])
                        tpc[u] = P.dma("a", ds["uT_s"].ap()[s, tt * 128:(tt + 1) * 128,
                                                            mt * 128:(mt + 1) * 128],
                                       stg[u % 2][:], deps=[tec])
                        u += 1
        P.barrier()

        CH2 = min(512, T)
        NCH2 = T // CH2
        with ExitStack() as px:
            XS = lambda n, sh, d=F32: px.enter_context(nc.sbuf_tensor(U(n), sh, d))
            xn1T = XS("e2_xn1T", [128, 8 * T], BF16)
            wsl = XS("e2_wsl", [128, 8 * 1024], BF16)
            wsl2 = [XS(f"e2_wsl2{i}", [128, 32 * 512], BF16) for i in range(2)]
            wsd = [XS(f"e2_wsd{i}", [128, D], BF16) for i in range(2)]
            r1 = XS("e2_r1", [128, 32 * CH2], BF16)
            r2 = XS("e2_r2", [128, 32 * CH2], BF16)
            g1c = XS("e2_g1c", [128, 8 * CH2])
            stg = [XS(f"e2_stg{i}", [128, 128]) for i in range(2)]
            txn = [P.dma("s", xn1T[:, j * T:(j + 1) * T], ds["xn1T_s"].ap()[s, j]) for j in range(8)]
            prev_end = None
            for cc in range(NCH2):
                c0 = cc * CH2
                t_r1 = {}
                t_r2 = {}
                with ExitStack() as pxa:
                    ps_h = [pxa.enter_context(nc.psum_tensor(U(f"e2_ph{i}"), [128, CH2], F32)) for i in range(2)]
                    for og in range(4):
                        twl = [P.dma("s", wsl[:, kc * 1024:(kc + 1) * 1024],
                                     ds["A0_s"].ap()[kc, :, og * 1024:(og + 1) * 1024],
                                     deps=[t_r1.get(og * 8 - 1), prev_end]) for kc in range(8)]
                        for m8 in range(8):
                            mt = og * 8 + m8
                            par = mt % 2

                            def h1mm(t, m8=m8, par=par, c0=c0):
                                for kc in range(8):
                                    for n0, nn in nsplit(CH2, 512):
                                        mm = t.matmul(ps_h[par][:, n0:n0 + nn],
                                                      wsl[:, kc * 1024 + m8 * 128: kc * 1024 + (m8 + 1) * 128],
                                                      xn1T[:, kc * T + c0 + n0: kc * T + c0 + n0 + nn],
                                                      start=(kc == 0), stop=(kc == 7))
                                return mm
                            tmm = P.run("t", h1mm, deps=txn + twl + [t_r1.get(mt - 2)])
                            t_r1[mt] = P.run("a", lambda a, mt=mt, par=par: a.activation(
                                r1[:, mt * CH2:(mt + 1) * CH2], ps_h[par][:], ACT.Relu,
                                scale=sc_sb[:, 0:1]), deps=[tmm])
                    t_r1all = [t_r1[mt] for mt in range(32)]
                    for og in range(8):
                        wpar = og % 2
                        twl2 = [P.dma("s", wsl2[wpar][:, kc * 512:(kc + 1) * 512],
                                      ds["A1_s"].ap()[kc, :, og * 512:(og + 1) * 512],
                                      deps=[t_r2.get((og - 2) * 4 + 3)]) for kc in range(32)]
                        for m4 in range(4):
                            mt = og * 4 + m4
                            par = mt % 2

                            def h2mm(t, m4=m4, par=par, wpar=wpar):
                                for kc in range(32):
                                    for n0, nn in nsplit(CH2, 512):
                                        mm = t.matmul(ps_h[par][:, n0:n0 + nn],
                                                      wsl2[wpar][:, kc * 512 + m4 * 128: kc * 512 + (m4 + 1) * 128],
                                                      r1[:, kc * CH2 + n0: kc * CH2 + n0 + nn],
                                                      start=(kc == 0), stop=(kc == 31))
                                return mm
                            tmm = P.run("t", h2mm, deps=t_r1all + twl2 + [t_r2.get(mt - 2)])
                            t_r2[mt] = P.run("a", lambda a, mt=mt, par=par: a.activation(
                                r2[:, mt * CH2:(mt + 1) * CH2], ps_h[par][:], ACT.Relu,
                                scale=sc_sb[:, 1:2]), deps=[tmm])
                    t_r2all = [t_r2[mt] for mt in range(32)]
                tg1 = []
                with ExitStack() as pxa:
                    ps8 = [pxa.enter_context(nc.psum_tensor(U(f"e2_p8{i}"), [128, CH2], F32)) for i in range(8)]
                    twd = {}
                    tdmm = {}
                    for kc in range(32):
                        twd[kc] = P.dma("s", wsd[kc % 2][:], ds["A2_s"].ap()[kc],
                                        deps=[tdmm.get(kc - 2)] + (t_r2all if kc < 2 else []))

                        def dmm(t, kc=kc):
                            for mt in range(8):
                                for n0, nn in nsplit(CH2, 512):
                                    mm = t.matmul(ps8[mt][:, n0:n0 + nn],
                                                  wsd[kc % 2][:, mt * 128:(mt + 1) * 128],
                                                  r2[:, kc * CH2 + n0: kc * CH2 + n0 + nn],
                                                  start=(kc == 0), stop=(kc == 31))
                            return mm
                        tdmm[kc] = P.run("t", dmm, deps=[twd[kc]] + t_r2all)
                    tg1 = [P.run("a", lambda a, mt=mt: a.activation(
                        g1c[:, mt * CH2:(mt + 1) * CH2], ps8[mt][:], ACT.Silu,
                        scale=rsw_sb[:, 3:4]), deps=[tdmm[31]]) for mt in range(8)]
                with ExitStack() as pxa:
                    ps_tr = [pxa.enter_context(nc.psum_tensor(U(f"e2_ptr{i}"), [128, 128], F32)) for i in range(2)]
                    u = 0
                    tpc = {}
                    for mt in range(8):
                        for tto in range(CH2 // 128):
                            ttr = P.run("t", lambda t, mt=mt, tto=tto, u=u: t.transpose(
                                ps_tr[u % 2][:], g1c[:, mt * CH2 + tto * 128: mt * CH2 + (tto + 1) * 128],
                                eye_f[:]), deps=[tg1[mt], tpc.get(u - 2)] + (tg1 if u < 2 else []))
                            tec = P.run("v", lambda v, u=u: v.tensor_copy(stg[u % 2][:], ps_tr[u % 2][:]),
                                        deps=[ttr])
                            tt_abs = c0 + tto * 128
                            tpc[u] = P.dma("a", ds["g1T_s"].ap()[s, tt_abs:tt_abs + 128,
                                                                 mt * 128:(mt + 1) * 128],
                                           stg[u % 2][:], deps=[tec])
                            u += 1
                    prev_end = tpc[u - 1]
                P.barrier()
        P.barrier()

        with ExitStack() as px:
            XS = lambda n, sh, d=F32: px.enter_context(nc.sbuf_tensor(U(n), sh, d))
            x1l = [XS(f"e3_x1{i}", [128, D]) for i in range(2)]
            utl = [XS(f"e3_ut{i}", [128, D]) for i in range(2)]
            g1l = [XS(f"e3_g1{i}", [128, D]) for i in range(2)]
            cen = XS("e3_cen", [128, D])
            sc2 = XS("e3_sc2", [128, D])
            stt = XS("e3_stt", [128, 8])
            rr = [XS(f"e3_r{i}", [128, D]) for i in range(2)]
            t_done = {}
            for tt in range(TT):
                par = tt % 2
                dp = [t_done.get(tt - 2)]
                tl1 = P.dma("s", x1l[par][:], ds["x1_s"].ap()[s, tt * 128:(tt + 1) * 128, :], deps=dp)
                tl2 = P.dma("s", utl[par][:], ds["uT_s"].ap()[s, tt * 128:(tt + 1) * 128, :], deps=dp)
                tl3 = P.dma("s", g1l[par][:], ds["g1T_s"].ap()[s, tt * 128:(tt + 1) * 128, :], deps=dp)
                tln = ln_chain(P, x1l[par], cen, sc2, stt, [tl1, t_done.get(tt - 2)])
                ts1 = P.run("v", lambda v, par=par: v.tensor_add(cen[:], cen[:], utl[par][:]),
                            deps=[tln, tl2])
                tr = P.run("v", lambda v, par=par: v.tensor_mul(rr[par][:], cen[:], g1l[par][:]),
                           deps=[ts1, tl3])
                tx2 = P.run("v", lambda v, par=par: v.tensor_add(rr[par][:], rr[par][:], x1l[par][:]),
                            deps=[tr])
                t_done[tt] = P.dma("s", out_h.ap()[s, tt * 128:(tt + 1) * 128, :], rr[par][:],
                                   deps=[tx2])
            P.wait("s", [t_done[TT - 1]])
        P.barrier()

    ctx.close()
    return nc, din, ds, out_h


_NC_CACHE = {}


def kernel(**inputs):
    SEQ = 16 // NCORES
    T = 2048
    key = (SEQ, T)
    if key not in _NC_CACHE:
        _NC_CACHE[key] = build(SEQ, T)
    nc, din, ds, out_h = _NC_CACHE[key]
    base = {}
    for name, arr in inputs.items():
        a = np.ascontiguousarray(arr)
        if a.dtype == np.bool_:
            a = a.view(np.uint8)
        if name in ("L1_ffn_s0", "L1_ffn_s1", "L1_ffn_s2"):
            a = a.reshape(1, 1)
        base[name] = a
    x = base.pop("L0_attn_x")
    in_maps = []
    for c in range(NCORES):
        m = dict(base)
        m["L0_attn_x"] = x[c * SEQ:(c + 1) * SEQ]
        in_maps.append(m)
    res = run_bass_kernel_spmd(nc, in_maps, core_ids=list(range(NCORES)))
    return np.concatenate([r["out"] for r in res.results], axis=0)


# version 15 (final)
# speedup vs baseline: 1.2009x; 1.0251x over previous
import numpy as np
import ml_dtypes
from contextlib import ExitStack

import concourse.bass as bass
import concourse.mybir as mybir
from concourse.bass_utils import run_bass_kernel_spmd

F32 = mybir.dt.float32
BF16 = mybir.dt.bfloat16
U8 = mybir.dt.uint8
AX = mybir.AxisListType
ALU = mybir.AluOpType
ACT = mybir.ActivationFunctionType

D = 1024
NH = 16
HD = 64
D3 = 3 * D
INNER = 4096
NST = 128
HC = 512
G3 = 3 * HC
NM = G3 // 128
LN_EPS = 1e-5
NCORES = 8

ENGMAP = {"t": "tensor", "v": "vector", "a": "scalar", "s": "sync", "g": "gpsimd"}


def nsplit(total, step):
    return [(i, min(step, total - i)) for i in range(0, total, step)]


class Prog:

    NPOOL = {"sd": 10, "ad": 10, "gd": 2}

    def __init__(self, nc, ctx):
        self.nc = nc
        self.keys = ["t", "v", "a"]
        for base, n in self.NPOOL.items():
            self.keys += [f"{base}{i}" for i in range(n)]
        self.sem = {k: ctx.enter_context(nc.semaphore("tk_" + k)) for k in self.keys}
        self.cnt = {k: 0 for k in self.keys}
        self.rot = {"sd": 0, "ad": 0, "gd": 0}
        self.wm = {e: {k: 0 for k in self.keys} for e in ENGMAP}
        self.q = {e: [] for e in ENGMAP}
        self.blk = None

    def barrier(self):
        for e, base in [("s", "sd"), ("a", "ad"), ("g", "gd")]:
            deps = [(f"{base}{i}", self.cnt[f"{base}{i}"]) for i in range(self.NPOOL[base])]
            if any(v > self.wm[e][k] for k, v in deps):
                self.wait(e, deps)
        self.flush()
        self.nc.all_engine_barrier()
        for e in ENGMAP:
            for k in self.keys:
                self.wm[e][k] = self.cnt[k]

    def _waits(self, eng, e, deps):
        best = {}
        for d in deps:
            if d is None:
                continue
            k, v = d
            if v > best.get(k, 0):
                best[k] = v
        for k, v in sorted(best.items()):
            if v > self.wm[e][k]:
                eng.wait_ge(self.sem[k], v)
                self.wm[e][k] = v

    def _calc_waits(self, e, deps):
        best = {}
        for d in deps:
            if d is None:
                continue
            k, v = d
            if v > best.get(k, 0):
                best[k] = v
        out = []
        for k, v in sorted(best.items()):
            if v > self.wm[e][k]:
                out.append((k, v))
                self.wm[e][k] = v
        return out

    def run(self, e, fn, deps=()):
        semkey = e
        waits = self._calc_waits(e, deps)
        self.cnt[semkey] += 1
        tick = (semkey, self.cnt[semkey])

        def play(eng, waits=waits, fn=fn, semkey=semkey):
            for (k, v) in waits:
                eng.wait_ge(self.sem[k], v)
            last = fn(eng)
            assert last is not None
            last.then_inc(self.sem[semkey], 1)

        self.q[e].append(play)
        return tick

    def dma(self, e, out, in_, deps=(), transpose=False):
        base = {"s": "sd", "a": "ad", "g": "gd"}[e]
        k = self.rot[base]
        self.rot[base] = (k + 1) % self.NPOOL[base]
        semkey = f"{base}{k}"
        prev = (semkey, self.cnt[semkey])
        waits = self._calc_waits(e, list(deps) + [prev])
        self.cnt[semkey] += 16
        tick = (semkey, self.cnt[semkey])

        def play(eng, waits=waits, out=out, in_=in_, transpose=transpose, semkey=semkey):
            for (kk, v) in waits:
                eng.wait_ge(self.sem[kk], v)
            if transpose:
                instr = eng.dma_start_transpose(out, in_)
            else:
                instr = eng.dma_start(out, in_)
            instr.then_inc(self.sem[semkey], 16)

        self.q[e].append(play)
        return tick

    def wait(self, e, deps):
        waits = self._calc_waits(e, deps)

        def play(eng, waits=waits):
            for (k, v) in waits:
                eng.wait_ge(self.sem[k], v)

        self.q[e].append(play)

    def flush(self):
        if not any(self.q.values()):
            return
        qs = {e: lst for e, lst in self.q.items() if lst}
        self.q = {e: [] for e in ENGMAP}
        with self.nc.Block() as blk:
            for e, lst in qs.items():
                def body(eng, lst=lst):
                    for play in lst:
                        play(eng)
                getattr(blk, ENGMAP[e])(body)


def col1(ap_1d, lo, hi):
    return ap_1d[lo:hi].rearrange("(p o) -> p o", o=1)


def build(SEQ, T):
    nc = bass.Bass("TRN2", target_bir_lowering=False)
    _uid = [0]

    def U(n):
        _uid[0] += 1
        return f"{n}_{_uid[0]}"
    TT = T // 128
    S4 = 4 * SEQ

    din = {}
    for name, shape, dt in [
        ("L0_attn_x", [SEQ, T, D], F32),
        ("L0_attn_W_qkv", [D, D3], F32), ("L0_attn_b_qkv", [D3], F32),
        ("L0_attn_W_proj", [D, D], F32), ("L0_attn_b_proj", [D], F32),
        ("L0_ffn_w_gate", [INNER, D], F32), ("L0_ffn_w_up", [INNER, D], F32),
        ("L0_ffn_w_down", [D, INNER], F32),
        ("L1_attn_A_log", [NST], F32), ("L1_attn_B_w", [NST, D], F32),
        ("L1_attn_B_b", [NST], F32), ("L1_attn_C_w", [D, NST], F32),
        ("L1_attn_C_b", [D], F32), ("L1_attn_norm_g", [D], F32),
        ("L1_attn_norm_b", [D], F32),
        ("L1_attn_ff1_w", [HC, D + HC], F32), ("L1_attn_ff1_b", [HC], F32),
        ("L1_attn_ff2_w", [HC, D + HC], F32), ("L1_attn_ff2_b", [HC], F32),
        ("L1_attn_ta_w", [HC, D + HC], F32), ("L1_attn_ta_b", [HC], F32),
        ("L1_attn_tb_w", [HC, D + HC], F32), ("L1_attn_tb_b", [HC], F32),
        ("L1_attn_out_w", [D, HC], F32), ("L1_attn_out_b", [D], F32),
        ("L1_ffn_k0", [2 * D, INNER], U8), ("L1_ffn_k1", [INNER, INNER], U8),
        ("L1_ffn_k2", [INNER, D], U8),
        ("L1_ffn_s0", [1, 1], F32), ("L1_ffn_s1", [1, 1], F32), ("L1_ffn_s2", [1, 1], F32),
    ]:
        din[name] = nc.dram_tensor(name, shape, dt, kind="ExternalInput")
    out_h = nc.dram_tensor("out", [SEQ, T, D], F32, kind="ExternalOutput")

    ds = {}
    for name, shape, dt in [
        ("wqkv_b", [8, 128, D3], BF16), ("wproj_b", [8, 128, D], BF16),
        ("wqn_g", [32, 128, D], BF16), ("wqn_u", [32, 128, D], BF16),
        ("wqn_d", [8, 128, INNER], BF16),
        ("wqT_g", [8, 128, INNER], BF16), ("wqT_u", [8, 128, INNER], BF16),
        ("wqT_d", [32, 128, D], BF16),
        ("A0_s", [8, 128, INNER], BF16), ("A1_s", [32, 128, INNER], BF16),
        ("A2_s", [32, 128, D], BF16),
        ("PxT_s", [8, 128, G3], BF16), ("VhT_s", [4, 128, G3], BF16),
        ("outwT_s", [4, 128, D], BF16),
        ("q_s", [SEQ, 8, 128, T], BF16), ("k_s", [SEQ, 8, 128, T], BF16),
        ("v_s", [SEQ, TT, 128, NH * (HD + 1)], BF16),
        ("s0_s", [SEQ, 8, 128, T], F32),
        ("x1_s", [SEQ, T, D], F32),
        ("xn1T_s", [SEQ, 8, 128, T], BF16),
        ("xniT_s", [SEQ, 8, 128, T], BF16),
        ("P_s", [128, NM, SEQ, T], F32),
        ("apow_d", [128, T], F32), ("asafe_d", [128, T], F32),
        ("cfc_s", [128, T * S4], BF16),
        ("uT_s", [SEQ, T, D], F32),
        ("g1T_s", [SEQ, T, D], F32),
    ]:
        ds[name] = nc.dram_tensor(name, shape, dt, kind="Internal")

    eye_b_h = nc.inline_tensor(np.eye(128).astype(ml_dtypes.bfloat16), name="c_eye_b")
    eye_f_h = nc.inline_tensor(np.eye(128, dtype=np.float32), name="c_eye_f")
    ones64_h = nc.inline_tensor(np.ones((64, 1)).astype(ml_dtypes.bfloat16), name="c_ones64")
    ones1_h = nc.inline_tensor(np.ones((1, 128), dtype=np.float32), name="c_ones1")
    trow_h = nc.inline_tensor(np.arange(T, dtype=np.float32)[None, :], name="c_trow")

    ctx = ExitStack()
    SB = lambda n, s, d=F32: ctx.enter_context(nc.sbuf_tensor(n, s, d))

    eye_b = SB("eye_b", [128, 128], BF16)
    eye_f = SB("eye_f", [128, 128], F32)
    ones64 = SB("ones64", [64, 1], BF16)
    ones1 = SB("ones1", [1, 128], F32)
    bqkv_sb = SB("bqkv_sb", [128, 16])
    bvbc = SB("bvbc", [128, D])
    bproj_sb = SB("bproj_sb", [128, 8])
    Bb_sb = SB("Bb_sb", [128, 1])
    Cb_sb = SB("Cb_sb", [128, 8])
    outb_sb = SB("outb_sb", [128, 8])
    gam_sb = SB("gam_sb", [128, 8])
    bet_sb = SB("bet_sb", [128, 8])
    Pb_sb = SB("Pb_sb", [128, NM])
    lnA_sb = SB("lnA_sb", [128, 1])
    rsw_sb = SB("rsw_sb", [128, 5])
    sc_sb = SB("sc_sb", [128, 2])
    BwT_sb = SB("BwT_sb", [128, 8 * 128], BF16)
    CwT_sb = SB("CwT_sb", [128, 8 * 128], BF16)
    inf_sb = SB("inf_sb", [1, SEQ])
    h0_sb = SB("h0_sb", [128, S4], BF16)
    half_sb = SB("half_sb", [128, 1])

    P = Prog(nc, ctx)

    with ExitStack() as px:
        XS = lambda n, s, d=F32: px.enter_context(nc.sbuf_tensor(n, s, d))
        tload = []
        for sb_t, h in [(eye_b, eye_b_h), (eye_f, eye_f_h), (ones64, ones64_h),
                        (ones1, ones1_h)]:
            tload.append(P.dma("s", sb_t[:], h.ap()))
        t_ones = tload[3]
        bq = din["L0_attn_b_qkv"].ap()
        for mt in range(16):
            P.dma("s", bqkv_sb[:, mt:mt + 1], col1(bq, mt * 128, (mt + 1) * 128))
        for mt in range(8):
            P.dma("s", bproj_sb[:, mt:mt + 1], col1(din["L0_attn_b_proj"].ap(), mt * 128, (mt + 1) * 128))
            P.dma("s", Cb_sb[:, mt:mt + 1], col1(din["L1_attn_C_b"].ap(), mt * 128, (mt + 1) * 128))
            P.dma("s", outb_sb[:, mt:mt + 1], col1(din["L1_attn_out_b"].ap(), mt * 128, (mt + 1) * 128))
            P.dma("s", gam_sb[:, mt:mt + 1], col1(din["L1_attn_norm_g"].ap(), mt * 128, (mt + 1) * 128))
            P.dma("s", bet_sb[:, mt:mt + 1], col1(din["L1_attn_norm_b"].ap(), mt * 128, (mt + 1) * 128))
        P.dma("s", Bb_sb[:, 0:1], col1(din["L1_attn_B_b"].ap(), 0, NST))
        tA = P.dma("s", lnA_sb[:, 0:1], col1(din["L1_attn_A_log"].ap(), 0, NST))
        for j in range(4):
            P.dma("s", Pb_sb[:, j:j + 1], col1(din["L1_attn_ff1_b"].ap(), j * 128, (j + 1) * 128))
            P.dma("s", Pb_sb[:, 4 + j:5 + j], col1(din["L1_attn_ff2_b"].ap(), j * 128, (j + 1) * 128))
        tmp4 = XS("w0_tmp4", [128, 4])
        t_ta = [P.dma("s", tmp4[:, j:j + 1], col1(din["L1_attn_ta_b"].ap(), j * 128, (j + 1) * 128)) for j in range(4)]
        t_tb = [P.dma("s", Pb_sb[:, 8 + j:9 + j], col1(din["L1_attn_tb_b"].ap(), j * 128, (j + 1) * 128)) for j in range(4)]
        P.run("v", lambda v: [v.tensor_single_scalar(Pb_sb[:, 8:12], Pb_sb[:, 8:12], 0.5, ALU.mult),
                              v.drain(),
                              v.scalar_tensor_tensor(Pb_sb[:, 8:12], tmp4[:], 0.05, Pb_sb[:, 8:12],
                                                     ALU.mult, ALU.add)][-1], deps=t_ta + t_tb)
        P.run("v", lambda v: v.memset(h0_sb[:], 0.0))
        P.run("v", lambda v: v.memset(half_sb[:], 0.5))
        t2 = P.run("a", lambda a: a.activation(lnA_sb[:, 0:1], lnA_sb[:, 0:1], ACT.Sigmoid), deps=[tA])
        t3 = P.run("a", lambda a: a.activation(lnA_sb[:, 0:1], lnA_sb[:, 0:1], ACT.Ln), deps=[t2])
        bv1 = XS("w0_bv", [1, D])
        tbv = P.dma("s", bv1[:], bq[2 * D:3 * D].rearrange("(o f) -> o f", o=1))
        ps_b = px.enter_context(nc.psum_tensor(U("w0_ps"), [128, D], F32))
        tms = [P.run("t", lambda t, n0=n0, nn=nn: t.matmul(ps_b[:, n0:n0 + nn], ones1[:],
                                                           bv1[:, n0:n0 + nn], start=True, stop=True),
                     deps=[tbv, t_ones]) for n0, nn in nsplit(D, 512)]
        P.run("v", lambda v: v.tensor_copy(bvbc[:], ps_b[:]), deps=tms)
        trow_sb = XS("w0_trow", [1, T])
        s3 = XS("w0_s3", [1, 3])
        apw = XS("w0_apw", [128, T])
        ttr = P.dma("s", trow_sb[:], trow_h.ap())
        tss = [P.dma("s", s3[:, i:i + 1], din[f"L1_ffn_s{i}"].ap()) for i in range(3)]
        tsm = P.run("v", lambda v: v.tensor_scalar_mul(s3[:], s3[:], 2.0), deps=tss)
        ps_b2 = px.enter_context(nc.psum_tensor(U("w0_ps2"), [128, T], F32))
        ps_s3 = px.enter_context(nc.psum_tensor(U("w0_ps3"), [128, 3], F32))
        tms = [P.run("t", lambda t, n0=n0, nn=nn: t.matmul(ps_b2[:, n0:n0 + nn], ones1[:],
                                                           trow_sb[:, n0:n0 + nn], start=True, stop=True),
                     deps=[ttr, t_ones]) for n0, nn in nsplit(T, 512)]
        tsp = P.run("t", lambda t: t.matmul(ps_s3[:], ones1[:], s3[:], start=True, stop=True), deps=[tsm])
        tap = P.run("a", lambda a: a.activation(apw[:], ps_b2[:], ACT.Exp, scale=lnA_sb[:, 0:1]),
                    deps=tms + [t3])
        P.run("v", lambda v: v.tensor_copy(sc_sb[:], ps_s3[:, 0:2]), deps=[tsp])
        P.run("v", lambda v: v.tensor_copy(rsw_sb[:, 3:4], ps_s3[:, 2:3]), deps=[tsp])
        tap2 = P.dma("s", ds["apow_d"].ap(), apw[:], deps=[tap])
        tmx = P.run("v", lambda v: v.tensor_scalar_max(apw[:], apw[:], 1e-8), deps=[tap, tap2])
        P.dma("s", ds["asafe_d"].ap(), apw[:], deps=[tmx])
    P.barrier()

    with ExitStack() as px:
        wb = [px.enter_context(nc.sbuf_tensor(U(f"w1_wb{i}"), [128, D3], F32)) for i in range(2)]
        wc = [px.enter_context(nc.sbuf_tensor(U(f"w1_wc{i}"), [128, D3], BF16)) for i in range(2)]
        jobs = [("L0_attn_W_qkv", "wqkv_b", D3, kc) for kc in range(8)] + \
               [("L0_attn_W_proj", "wproj_b", D, kc) for kc in range(8)]
        tv = {}
        tst = {}
        for ji, (src, dstn, ncol, kc) in enumerate(jobs):
            tl = P.dma("s", wb[ji % 2][:, 0:ncol], din[src].ap()[kc * 128:(kc + 1) * 128, :],
                       deps=[tv.get(ji - 2)])
            tv[ji] = P.run("v", lambda v, ji=ji, ncol=ncol: v.tensor_copy(wc[ji % 2][:, 0:ncol],
                                                                          wb[ji % 2][:, 0:ncol]),
                           deps=[tl, tst.get(ji - 2)])
            tst[ji] = P.dma("a", ds[dstn].ap()[kc], wc[ji % 2][:, 0:ncol], deps=[tv[ji]])
    P.barrier()

    W2J = [("L0_ffn_w_gate", "wqn_g", 32, D, 0),
           ("L0_ffn_w_up", "wqn_u", 32, D, 1),
           ("L0_ffn_w_down", "wqn_d", 8, INNER, 2)]
    with ExitStack() as px:
        wb = {}
        red = {}
        acc = {}
        t1s = {}
        gbit = {}
        sgn = {}
        wq = {}
        swv = {}
        ps_t = {}
        ps_sp = {}
        for _, _, nchunks, ncols, r in W2J:
            wb[r] = [px.enter_context(nc.sbuf_tensor(U(f"w2_wb{r}{i}"), [128, ncols], F32)) for i in range(2)]
            red[r] = px.enter_context(nc.sbuf_tensor(U(f"w2_red{r}"), [128, 1], F32))
            acc[r] = px.enter_context(nc.sbuf_tensor(U(f"w2_acc{r}"), [128, 1], F32))
            t1s[r] = px.enter_context(nc.sbuf_tensor(U(f"w2_t1{r}"), [1, 1], F32))
            gbit[r] = px.enter_context(nc.sbuf_tensor(U(f"w2_g{r}"), [128, ncols], F32))
            sgn[r] = [px.enter_context(nc.sbuf_tensor(U(f"w2_s{r}{i}"), [128, ncols], F32)) for i in range(2)]
            wq[r] = [px.enter_context(nc.sbuf_tensor(U(f"w2_wq{r}{i}"), [128, ncols], BF16)) for i in range(2)]
            swv[r] = px.enter_context(nc.sbuf_tensor(U(f"w2_sw{r}"), [128, 1], F32))
            ps_t[r] = px.enter_context(nc.psum_tensor(U(f"w2_ps{r}"), [1, 128], F32))
            ps_sp[r] = px.enter_context(nc.psum_tensor(U(f"w2_ps2{r}"), [128, 1], F32))
        tred = {}
        for _, _, _, _, r in W2J:
            tred[r] = P.run("v", lambda v, r=r: v.memset(acc[r][:], 0.0))
        for c in range(32):
            for wname, _, nchunks, _, r in W2J:
                if c >= nchunks:
                    continue
                tl = P.dma("s", wb[r][c % 2][:], din[wname].ap()[c * 128:(c + 1) * 128, :],
                           deps=[tred[r]] if c >= 2 else [])
                tr = P.run("v", lambda v, r=r, c=c: v.reduce_sum(red[r][:], wb[r][c % 2][:], AX.X,
                                                                 apply_absolute_value=True),
                           deps=[tl, tred[r]])
                tred[r] = P.run("v", lambda v, r=r: v.tensor_add(acc[r][:], acc[r][:], red[r][:]),
                                deps=[tr])
        tsw = {}
        for _, _, nchunks, ncols, r in W2J:
            ttr = P.run("t", lambda t, r=r: t.transpose(ps_t[r][:], acc[r][:, 0:1], eye_f[:]),
                        deps=[tred[r]])
            tt1 = P.run("v", lambda v, r=r: v.reduce_sum(t1s[r][:], ps_t[r][0:1, :], AX.X), deps=[ttr])
            tt2 = P.run("v", lambda v, r=r, nchunks=nchunks, ncols=ncols: v.tensor_scalar(
                t1s[r][:], t1s[r][:], 1.0 / (nchunks * 128 * ncols), 1e-5, ALU.mult, ALU.max),
                deps=[tt1])
            tsp = P.run("t", lambda t, r=r: t.matmul(ps_sp[r][:], ones1[:], t1s[r][:],
                                                     start=True, stop=True), deps=[tt2])
            tsw[r] = P.run("v", lambda v, r=r: v.reciprocal(swv[r][:], ps_sp[r][:]), deps=[tsp])
            P.run("v", lambda v, r=r: v.tensor_copy(rsw_sb[:, r:r + 1], ps_sp[r][:]), deps=[tsp])
        tst = {r: {} for _, _, _, _, r in W2J}
        tm = {r: None for _, _, _, _, r in W2J}
        for c in range(32):
            for wname, spill, nchunks, _, r in W2J:
                if c >= nchunks:
                    continue
                tl = P.dma("s", wb[r][c % 2][:], din[wname].ap()[c * 128:(c + 1) * 128, :],
                           deps=[tst[r].get(c - 2), tsw[r]])
                tsg = P.run("a", lambda a, r=r, c=c: a.activation(sgn[r][c % 2][:], wb[r][c % 2][:],
                                                                  ACT.Sign), deps=[tl])
                tg0 = P.run("a", lambda a, r=r, c=c: a.activation(gbit[r][:], wb[r][c % 2][:],
                                                                  ACT.Abs), deps=[tl, tm[r]])
                tg = P.run("v", lambda v, r=r: v.tensor_scalar_mul(gbit[r][:], gbit[r][:],
                                                                   swv[r][:, 0:1]), deps=[tg0])
                tg2 = P.run("v", lambda v, r=r: v.tensor_single_scalar(gbit[r][:], gbit[r][:], 0.5,
                                                                       ALU.is_ge), deps=[tg])
                tm[r] = P.run("v", lambda v, r=r, c=c: v.tensor_mul(wq[r][c % 2][:], gbit[r][:],
                                                                    sgn[r][c % 2][:]),
                              deps=[tg2, tsg, tst[r].get(c - 2)])
                tst[r][c] = P.dma("a", ds[spill].ap()[c], wq[r][c % 2][:], deps=[tm[r]])
    P.barrier()
    P.run("v", lambda v: v.tensor_mul(rsw_sb[:, 4:5], rsw_sb[:, 1:2], rsw_sb[:, 2:3]))

    with ExitStack() as px:
        tb = [px.enter_context(nc.sbuf_tensor(U(f"w3_tb{i}"), [128, INNER], BF16)) for i in range(2)]
        prev = {}
        i = 0
        for src, dstn, nj, ncols in [("wqn_g", "wqT_g", 8, INNER), ("wqn_u", "wqT_u", 8, INNER),
                                     ("wqn_d", "wqT_d", 32, D)]:
            for j in range(nj):
                tl = P.dma("s", tb[i % 2][:, 0:ncols],
                           ds[src].ap()[:, :, j * 128:(j + 1) * 128].rearrange("a b c -> (a b) c"),
                           deps=[prev.get(i - 2)], transpose=True)
                prev[i] = P.dma("a", ds[dstn].ap()[j], tb[i % 2][:, 0:ncols], deps=[tl])
                i += 1
    P.barrier()

    with ExitStack() as px:
        ub = [px.enter_context(nc.sbuf_tensor(U(f"w4_u{i}"), [128, INNER], U8)) for i in range(2)]
        ub2 = [px.enter_context(nc.sbuf_tensor(U(f"w4_u2{i}"), [128, INNER], U8)) for i in range(2)]
        cb = [px.enter_context(nc.sbuf_tensor(U(f"w4_c{i}"), [128, INNER], BF16)) for i in range(2)]
        tcv = {}
        tstv = {}
        i = 0
        for c in range(8):
            tl = P.dma("s", ub[i % 2][:], din["L1_ffn_k0"].ap()[c * 128:(c + 1) * 128, :],
                       deps=[tcv.get(i - 2)])
            tl2 = P.dma("s", ub2[i % 2][:], din["L1_ffn_k0"].ap()[D + c * 128:D + (c + 1) * 128, :],
                        deps=[tcv.get(i - 2)])
            tcv[i] = P.run("v", lambda v, i=i: v.tensor_sub(cb[i % 2][:], ub[i % 2][:], ub2[i % 2][:]),
                           deps=[tl, tl2, tstv.get(i - 2)])
            tstv[i] = P.dma("a", ds["A0_s"].ap()[c], cb[i % 2][:], deps=[tcv[i]])
            i += 1
        for src, dstn, nchunks, ncols in [("L1_ffn_k1", "A1_s", 32, INNER), ("L1_ffn_k2", "A2_s", 32, D)]:
            for c in range(nchunks):
                tl = P.dma("s", ub[i % 2][:, 0:ncols], din[src].ap()[c * 128:(c + 1) * 128, :],
                           deps=[tcv.get(i - 2)])
                tcv[i] = P.run("v", lambda v, i=i, ncols=ncols: v.tensor_scalar(
                    cb[i % 2][:, 0:ncols], ub[i % 2][:, 0:ncols], 2.0, -1.0, ALU.mult, ALU.add),
                    deps=[tl, tstv.get(i - 2)])
                tstv[i] = P.dma("a", ds[dstn].ap()[c], cb[i % 2][:, 0:ncols], deps=[tcv[i]])
                i += 1
    P.barrier()

    with ExitStack() as px:
        wb = px.enter_context(nc.sbuf_tensor(U("w5_wb"), [128, D + HC], F32))
        wb2 = px.enter_context(nc.sbuf_tensor(U("w5_wb2"), [128, D + HC], F32))
        cbw = px.enter_context(nc.sbuf_tensor(U("w5_cb"), [128, D + HC], BF16))
        stg = [px.enter_context(nc.sbuf_tensor(U(f"w5_st{i}"), [128, 128], BF16)) for i in range(2)]
        ps_t = [px.enter_context(nc.psum_tensor(U(f"w5_ps{i}"), [128, 128], BF16)) for i in range(2)]
        jobs = [("B_w", "L1_attn_B_w", 0, 0, D, "BwT")]
        jobs += [("C_w", "L1_attn_C_w", c, 0, NST, "CwT") for c in range(8)]
        for g, wname in enumerate(["L1_attn_ff1_w", "L1_attn_ff2_w", "VAB"]):
            jobs += [(wname, wname, c, g, D + HC, "PxVh") for c in range(4)]
        jobs += [("out_w", "L1_attn_out_w", c, 0, HC, "outwT") for c in range(8)]
        tcast = None
        tevac = {}
        u = 0
        for ji, (nm, wname, c, g, ncols, kind) in enumerate(jobs):
            ntile = (ncols + 127) // 128
            prev_trs = [tevac.get(u - jj) for jj in range(1, ntile + 1)]
            if nm == "VAB":
                tl1 = P.dma("s", wb[:, 0:ncols], din["L1_attn_ta_w"].ap()[c * 128:(c + 1) * 128, :],
                            deps=[tcast])
                tl2 = P.dma("s", wb2[:, 0:ncols], din["L1_attn_tb_w"].ap()[c * 128:(c + 1) * 128, :],
                            deps=[tcast])
                tl = P.run("v", lambda v, ncols=ncols: [
                    v.tensor_single_scalar(wb2[:, 0:ncols], wb2[:, 0:ncols], 0.5, ALU.mult),
                    v.drain(),
                    v.scalar_tensor_tensor(wb[:, 0:ncols], wb[:, 0:ncols], 0.05, wb2[:, 0:ncols],
                                           ALU.mult, ALU.add)][-1],
                    deps=[tl1, tl2])
            else:
                tl = P.dma("s", wb[:, 0:ncols], din[wname].ap()[c * 128:(c + 1) * 128, :], deps=[tcast])
            tcast = P.run("v", lambda v, ncols=ncols: v.tensor_copy(cbw[:, 0:ncols], wb[:, 0:ncols]),
                          deps=[tl] + prev_trs)
            for jt in range(ntile):
                w = min(128, ncols - jt * 128)
                ttr = P.run("t", lambda t, jt=jt, w=w, u=u: t.transpose(
                    ps_t[u % 2][:, 0:128], cbw[:, jt * 128:jt * 128 + w], eye_b[:]),
                    deps=[tcast, tevac.get(u - 2)])
                if kind == "BwT":
                    tevac[u] = P.run("v", lambda v, jt=jt, w=w, u=u: v.tensor_copy(
                        BwT_sb[0:w, jt * 128:(jt + 1) * 128], ps_t[u % 2][0:w, 0:128]), deps=[ttr])
                elif kind == "CwT":
                    tevac[u] = P.run("v", lambda v, c=c, w=w, u=u: v.tensor_copy(
                        CwT_sb[0:w, c * 128:(c + 1) * 128], ps_t[u % 2][0:w, 0:128]), deps=[ttr])
                else:
                    te = P.run("v", lambda v, w=w, u=u: v.tensor_copy(
                        stg[u % 2][0:w, 0:128], ps_t[u % 2][0:w, 0:128]), deps=[ttr])
                    if kind == "PxVh":
                        if jt < 8:
                            tgt = ds["PxT_s"].ap()[jt, :, g * HC + c * 128: g * HC + (c + 1) * 128]
                        else:
                            tgt = ds["VhT_s"].ap()[jt - 8, :, g * HC + c * 128: g * HC + (c + 1) * 128]
                    else:
                        tgt = ds["outwT_s"].ap()[jt, :, c * 128:(c + 1) * 128]
                    tevac[u] = P.dma("a", tgt, stg[u % 2][0:w, 0:128], deps=[te])
                u += 1
    P.barrier()

    def ln_chain(P, xin, cen, sc2, stt, deps):
        tr1 = P.run("v", lambda v: v.reduce_sum(stt[:, 0:1], xin[:], AX.X), deps=deps)
        tr2 = P.run("v", lambda v: v.tensor_scalar_mul(stt[:, 1:2], stt[:, 0:1], 1.0 / D), deps=[tr1])
        tr3 = P.run("v", lambda v: v.tensor_scalar_sub(cen[:], xin[:], stt[:, 1:2]), deps=[tr2])
        tr4 = P.run("v", lambda v: v.tensor_mul(sc2[:], cen[:], cen[:]), deps=[tr3])
        tr5 = P.run("v", lambda v: v.reduce_sum(stt[:, 2:3], sc2[:], AX.X), deps=[tr4])
        tr5b = P.run("v", lambda v: v.tensor_scalar(stt[:, 3:4], stt[:, 2:3], 1.0 / D, LN_EPS,
                                                    ALU.mult, ALU.add), deps=[tr5])
        tr6 = P.run("a", lambda a: a.activation(stt[:, 3:4], stt[:, 3:4], ACT.Sqrt), deps=[tr5b])
        tr6b = P.run("v", lambda v: v.reciprocal(stt[:, 3:4], stt[:, 3:4]), deps=[tr6])
        tr7 = P.run("v", lambda v: v.tensor_scalar_mul(cen[:], cen[:], stt[:, 3:4]), deps=[tr6b])
        return tr7

    def quant_chain(P, xin, q1, qi, outb, ncol, stt, deps):
        tq0 = P.run("v", lambda v: v.reduce_max(stt[:, 4:5], xin[:, 0:ncol], AX.X,
                                                apply_absolute_value=True), deps=deps)
        tq1 = P.run("v", lambda v: v.tensor_scalar_max(stt[:, 4:5], stt[:, 4:5], 1e-5), deps=[tq0])
        tq2 = P.run("v", lambda v: v.reciprocal(stt[:, 5:6], stt[:, 4:5]), deps=[tq1])
        tq3 = P.run("v", lambda v: v.tensor_scalar_mul(stt[:, 6:7], stt[:, 5:6], 127.0), deps=[tq2])
        tq4 = P.run("v", lambda v: v.tensor_scalar_mul(stt[:, 7:8], stt[:, 4:5], 1.0 / 127.0), deps=[tq2])
        tqb = P.run("v", lambda v: [v.tensor_scalar_mul(stt[:, 8:9], stt[:, 7:8], -16384.0),
                                    v.drain(),
                                    v.memset(stt[:, 9:10], 16384.0)][-1], deps=[tq4])
        ty = P.run("a", lambda a: a.activation(q1[:, 0:ncol], xin[:, 0:ncol], ACT.Identity,
                                               bias=stt[:, 9:10], scale=stt[:, 6:7]),
                   deps=[tq3, tqb])
        tc1 = P.run("v", lambda v: v.tensor_copy(qi[:, 0:ncol], q1[:, 0:ncol]), deps=[ty])
        td = P.run("v", lambda v: v.tensor_sub(q1[:, 0:ncol], q1[:, 0:ncol], qi[:, 0:ncol]), deps=[tc1])
        tg = P.run("v", lambda v: v.tensor_single_scalar(q1[:, 0:ncol], q1[:, 0:ncol], 0.5, ALU.is_ge),
                   deps=[td])
        tr2 = P.run("v", lambda v: v.tensor_add(q1[:, 0:ncol], q1[:, 0:ncol], qi[:, 0:ncol]), deps=[tg])
        to = P.run("a", lambda a: a.activation(outb[:, 0:ncol], q1[:, 0:ncol], ACT.Identity,
                                               bias=stt[:, 8:9], scale=stt[:, 7:8]),
                   deps=[tr2, tqb])
        return to

    for s in range(SEQ):
        with ExitStack() as px:
            XS = lambda n, sh, d=F32: px.enter_context(nc.sbuf_tensor(U(n), sh, d))
            xt = [XS(f"a_x{i}", [128, D]) for i in range(2)]
            cen = [XS(f"a_cen{i}", [128, D]) for i in range(2)]
            sc2 = XS("a_sc2", [128, D])
            stt = XS("a_stt", [128, 8])
            xnb = [XS(f"a_xnb{i}", [128, D], BF16) for i in range(2)]
            xnT = XS("a_xnT", [128, 8 * T], BF16)
            wq_sb = XS("a_wq", [128, 8 * D3], BF16)
            iacc = XS("a_iacc", [128, 8])
            imax = XS("a_imax", [128, 1])
            qkst = [XS(f"a_qkst{i}", [128, T], BF16) for i in range(2)]
            vst = [XS(f"a_vst{i}", [128, NH * (HD + 1)], BF16) for i in range(2)]
            ps_tr = [px.enter_context(nc.psum_tensor(U(f"a_ptr{i}"), [128, 128], BF16)) for i in range(2)]
            HW2 = min(1024, T)
            ps_qk = [px.enter_context(nc.psum_tensor(U(f"a_pqk{i}"), [128, max(HW2, D)], F32)) for i in range(2)]
            ps_im = px.enter_context(nc.psum_tensor(U("a_pim"), [1, 128], F32))

            twq = [P.dma("s", wq_sb[:, kc * D3:(kc + 1) * D3], ds["wqkv_b"].ap()[kc]) for kc in range(8)]
            t_tr = {}
            t_ln = {}
            for tt in range(TT):
                par = tt % 2
                tl = P.dma("s", xt[par][:], din["L0_attn_x"].ap()[s, tt * 128:(tt + 1) * 128, :],
                           deps=[t_ln.get(tt - 2)])
                tln = ln_chain(P, xt[par], cen[par], sc2, stt,
                               [tl, t_ln.get(tt - 2)])
                t_ln[tt] = tcast = P.run("v", lambda v, par=par: v.tensor_copy(xnb[par][:], cen[par][:]),
                                         deps=[tln])
                for j in range(8):
                    u = tt * 8 + j
                    ttr = P.run("t", lambda t, par=par, j=j, u=u: t.transpose(
                        ps_tr[u % 2][:], xnb[par][:, j * 128:(j + 1) * 128], eye_b[:]),
                        deps=[tcast, t_tr.get(u - 2)])
                    t_tr[u] = P.run("v", lambda v, tt=tt, j=j, u=u: v.tensor_copy(
                        xnT[:, j * T + tt * 128: j * T + (tt + 1) * 128], ps_tr[u % 2][:]), deps=[ttr])
            t_xnT = [t_tr[u] for u in range(8 * TT)]
            tia = [P.run("v", lambda v, j=j: v.reduce_sum(iacc[:, j:j + 1], xnT[:, j * T:(j + 1) * T],
                                                          AX.X, apply_absolute_value=True), deps=t_xnT)
                   for j in range(8)]
            tim = P.run("v", lambda v: v.reduce_max(imax[:], iacc[:], AX.X), deps=tia)
            tit = P.run("t", lambda t: t.transpose(ps_im[:], imax[:, 0:1], eye_f[:]), deps=[tim])
            P.run("v", lambda v: v.reduce_max(inf_sb[:, s:s + 1], ps_im[0:1, :], AX.X), deps=[tit])

            t_ev = {}
            NHF = T // HW2
            u = 0
            for mt in range(16):
                tevs = []
                for hf in range(NHF):
                    par = u % 2

                    def qkmm(t, mt=mt, hf=hf, par=par):
                        for kc in range(8):
                            for n0, nn in nsplit(HW2, 512):
                                mm = t.matmul(ps_qk[par][:, n0:n0 + nn],
                                              wq_sb[:, kc * D3 + mt * 128: kc * D3 + (mt + 1) * 128],
                                              xnT[:, kc * T + hf * HW2 + n0: kc * T + hf * HW2 + n0 + nn],
                                              start=(kc == 0), stop=(kc == 7))
                        return mm
                    tmm = P.run("t", qkmm, deps=t_xnT + twq + [t_ev.get(u - 2)])
                    tev = P.run("v", lambda v, mt=mt, hf=hf, par=par: v.tensor_scalar_add(
                        qkst[mt % 2][:, hf * HW2:(hf + 1) * HW2], ps_qk[par][:, 0:HW2],
                        bqkv_sb[:, mt:mt + 1]), deps=[tmm])
                    tevs.append(tev)
                    t_ev[u] = tev
                    u += 1
                tgt = ds["q_s"] if mt < 8 else ds["k_s"]
                t_ev[u - 1] = P.dma("a", tgt.ap()[s, mt % 8], qkst[mt % 2][:], deps=tevs)
            tz = P.run("v", lambda v: [v.memset(vst[i][:, h * (HD + 1) + HD: h * (HD + 1) + HD + 1], 1.0)
                                       for i in range(2) for h in range(NH)][-1])
            t_vev = {}
            for tt in range(TT):
                par = tt % 2

                def vmm(t, tt=tt, par=par):
                    for kc in range(8):
                        for n0, nn in nsplit(D, 512):
                            mm = t.matmul(ps_qk[par][:, n0:n0 + nn],
                                          xnT[:, kc * T + tt * 128: kc * T + (tt + 1) * 128],
                                          wq_sb[:, kc * D3 + 2 * D + n0: kc * D3 + 2 * D + n0 + nn],
                                          start=(kc == 0), stop=(kc == 7))
                    return mm
                tmm = P.run("t", vmm, deps=[t_ev.get(16 * NHF - 2), t_ev.get(16 * NHF - 1),
                                            t_vev.get(tt - 2)])

                def vevac(v, tt=tt, par=par):
                    for h in range(NH):
                        mm = v.tensor_add(vst[par][:, h * (HD + 1): h * (HD + 1) + HD],
                                          ps_qk[par][:, h * HD:(h + 1) * HD],
                                          bvbc[:, h * HD:(h + 1) * HD])
                    return mm
                tev = P.run("v", vevac, deps=[tmm, tz, t_vev.get(tt - 2)])
                t_vev[tt] = P.dma("s", ds["v_s"].ap()[s, tt], vst[par][:], deps=[tev])
        P.barrier()

        with ExitStack() as px:
            XS = lambda n, sh, d=F32: px.enter_context(nc.sbuf_tensor(U(n), sh, d))
            q_ct = XS("at_q", [128, T], BF16)
            k_ct = XS("at_k", [128, T], BF16)
            scr = XS("at_scr", [128, T], BF16)
            red1 = XS("at_red", [64, 1])
            tt8 = XS("at_t8", [1, 8])
            splat = XS("at_spl", [128, 2 * NH])
            with ExitStack() as px2:
                ps_st = px2.enter_context(nc.psum_tensor(U("at_pst"), [64, T], F32))
                ps_tp = px2.enter_context(nc.psum_tensor(U("at_ptp"), [1, 64], F32))
                ps_sp = px2.enter_context(nc.psum_tensor(U("at_psp"), [128, 2], F32))
                tspl = {}
                prev_c1 = None
                prev_c2 = None
                prev_v1 = None
                prev_v2 = None
                prev_v4 = None
                for ct in range(8):
                    tq = P.dma("s", q_ct[:], ds["q_s"].ap()[s, ct], deps=[tspl.get(2 * ct - 1)])
                    tk = P.dma("s", k_ct[:], ds["k_s"].ap()[s, ct], deps=[tspl.get(2 * ct - 1)])
                    for hh in range(2):
                        h = ct * 2 + hh
                        p0 = 64 * hh
                        ta1 = P.run("a", lambda a, p0=p0: a.activation(scr[0:64, :], q_ct[p0:p0 + 64, :],
                                                                       ACT.Abs), deps=[tq, prev_c2])

                        def csum(t, row):
                            r0 = row * 32
                            for n0, nn in nsplit(T, 512):
                                mm = t.matmul(ps_st[r0:r0 + 1, n0:n0 + nn], ones64[:],
                                              scr[0:64, n0:n0 + nn], start=True, stop=True)
                            return mm
                        tc1 = P.run("t", lambda t: csum(t, 0), deps=[ta1, prev_v1])
                        tv1 = P.run("v", lambda v: v.reduce_max(tt8[:, 0:1], ps_st[0:1, :], AX.X),
                                    deps=[tc1])
                        ta2 = P.run("a", lambda a, p0=p0: a.activation(scr[0:64, :], q_ct[p0:p0 + 64, :],
                                                                       ACT.Square), deps=[tc1])
                        tc2 = P.run("t", lambda t: csum(t, 1), deps=[ta2, prev_v2])
                        prev_c1, prev_c2 = tc1, tc2
                        prev_v1, prev_v2 = tv1, None
                        tv2 = P.run("v", lambda v: v.reduce_sum(tt8[:, 1:2], ps_st[32:33, :], AX.X),
                                    deps=[tc2])
                        prev_v2 = tv2
                        tv3 = P.run("v", lambda v, p0=p0: v.reduce_sum(red1[:], k_ct[p0:p0 + 64, :], AX.X,
                                                                       apply_absolute_value=True),
                                    deps=[tk, prev_v4]) if True else None
                        tt3 = P.run("t", lambda t: t.transpose(ps_tp[:], red1[:, 0:1],
                                                               eye_f[0:64, 0:64]),
                                    deps=[tv3, prev_v4])
                        tv4 = P.run("v", lambda v: v.reduce_max(tt8[:, 2:3], ps_tp[0:1, :], AX.X),
                                    deps=[tt3])
                        prev_v4 = tv4
                        ta3 = P.run("a", lambda a: a.activation(tt8[:, 3:4], tt8[:, 1:2], ACT.Sqrt),
                                    deps=[tv2])
                        tv5 = P.run("v", lambda v: v.tensor_mul(tt8[:, 4:5], tt8[:, 3:4],
                                                                inf_sb[:, s:s + 1]), deps=[ta3])
                        tv6 = P.run("v", lambda v: v.reciprocal(tt8[:, 5:6], tt8[:, 4:5]), deps=[tv5])
                        tv7 = P.run("v", lambda v: v.tensor_scalar_mul(tt8[:, 6:7], tt8[:, 5:6], 2.0),
                                    deps=[tv6])
                        tv8 = P.run("v", lambda v: v.tensor_mul(tt8[:, 7:8], tt8[:, 0:1], tt8[:, 2:3]),
                                    deps=[tv1, tv4])
                        tv9 = P.run("v", lambda v: v.scalar_tensor_tensor(tt8[:, 7:8], tt8[:, 7:8], -2.0,
                                                                          tt8[:, 6:7], ALU.mult, ALU.mult),
                                    deps=[tv8, tv7])
                        tsp = P.run("t", lambda t: t.matmul(ps_sp[:], ones1[:], tt8[:, 6:8],
                                                            start=True, stop=True), deps=[tv9])
                        tspl[h] = P.run("v", lambda v, h=h: v.tensor_copy(splat[:, 2 * h:2 * h + 2], ps_sp[:]),
                                        deps=[tsp])
            v_full = XS("at_vf", [128, TT * NH * (HD + 1)], BF16)
            QW = min(1024, T)
            NU = T // QW
            attnT = [XS(f"at_aT{i}", [128, QW], BF16) for i in range(4)]
            qb2 = [XS(f"at_q2{i}", [128, T], BF16) for i in range(2)]
            kb2 = [XS(f"at_k2{i}", [128, T], BF16) for i in range(2)]
            av_f = XS("at_avf", [64, T])
            rz = XS("at_rz", [1, T])
            attn_fm = XS("at_fm", [128, 8 * T], BF16)
            ps_sc = px.enter_context(nc.psum_tensor(U("at_psc"), [128, T], F32))
            ps_av = px.enter_context(nc.psum_tensor(U("at_pav"), [128, T], F32))
            tvf = [P.dma("s", v_full[:, tt * NH * (HD + 1):(tt + 1) * NH * (HD + 1)],
                         ds["v_s"].ap()[s, tt]) for tt in range(TT)]
            t_fm = {}
            exp_tail = [None] * max(NU, 2)
            tlast_prev = None
            NUU = TT * NU
            for ct in range(8):
                qcb, kcb = qb2[ct % 2], kb2[ct % 2]
                tq = P.dma("s", qcb[:], ds["q_s"].ap()[s, ct],
                           deps=[t_fm.get(2 * ct - 3), tspl[15]])
                tk = P.dma("s", kcb[:], ds["k_s"].ap()[s, ct], deps=[t_fm.get(2 * ct - 3)])
                for hh in range(2):
                    h = ct * 2 + hh
                    p0 = 64 * hh
                    texp = {}
                    tavd = {}

                    def emit_av(w, kchunk, qh, h=h):
                        def avmm(t, kchunk=kchunk, qh=qh, h=h, w=w):
                            vof = kchunk * NH * (HD + 1) + h * (HD + 1)
                            for n0, nn in nsplit(QW, 512):
                                mm = t.matmul(ps_av[0:HD + 1, qh * QW + n0: qh * QW + n0 + nn],
                                              v_full[:, vof:vof + HD + 1],
                                              attnT[w % 4][:, n0:n0 + nn],
                                              start=(kchunk == 0), stop=(kchunk == TT - 1))
                            return mm
                        tavd[w] = P.run("t", avmm, deps=[texp[w]] + tvf +
                                        ([t_fm.get(h - 1)] if kchunk == 0 else []))

                    av_args = {}
                    for kchunk in range(TT):
                        for qh in range(NU):
                            u = kchunk * NU + qh

                            def smm(t, kchunk=kchunk, qh=qh, p0=p0, qcb=qcb, kcb=kcb):
                                for n0, nn in nsplit(QW, 512):
                                    mm = t.matmul(ps_sc[:, qh * QW + n0: qh * QW + n0 + nn],
                                                  kcb[p0:p0 + 64, kchunk * 128:(kchunk + 1) * 128],
                                                  qcb[p0:p0 + 64, qh * QW + n0: qh * QW + n0 + nn],
                                                  start=True, stop=True)
                                return mm
                            tmm = P.run("t", smm,
                                        deps=[tq, tk,
                                              texp.get(u - NU, exp_tail[u] if u < NU else None),
                                              tspl[h], t_fm.get(h - 1)])
                            texp[u] = P.run("a", lambda a, qh=qh, h=h, u=u: a.activation(
                                attnT[u % 4][:], ps_sc[:, qh * QW:(qh + 1) * QW], ACT.Exp,
                                bias=splat[:, 2 * h + 1:2 * h + 2], scale=splat[:, 2 * h:2 * h + 1]),
                                deps=[tmm, tavd.get(u - 4, tlast_prev if u < 4 else None)])
                            av_args[u] = (kchunk, qh)
                            if u >= 1:
                                emit_av(u - 1, *av_args[u - 1])
                    emit_av(NUU - 1, *av_args[NUU - 1])
                    tlast = tavd[NUU - 1]
                    trz = P.run("v", lambda v: v.reciprocal(rz[:], ps_av[HD:HD + 1, :]), deps=[tlast])
                    tae = P.run("v", lambda v: v.tensor_copy(av_f[:], ps_av[0:HD, :]), deps=[tlast])

                    def bcast(t):
                        for n0, nn in nsplit(T, 512):
                            mm = t.matmul(ps_av[:, n0:n0 + nn], ones1[:], rz[:, n0:n0 + nn],
                                          start=True, stop=True)
                        return mm
                    tbc = P.run("t", bcast, deps=[trz, tae])
                    t_fm[h] = P.run("v", lambda v, h=h, p0=p0: v.tensor_mul(
                        attn_fm[p0:p0 + 64, (h // 2) * T:(h // 2 + 1) * T], av_f[:], ps_av[0:64, :]),
                        deps=[tbc, tae])
                    exp_tail = [texp[NUU - NU + i] for i in range(NU)]
                    tlast_prev = tlast
            wp_sb = XS("at_wp", [128, 8 * D], BF16)
            s0st = [XS(f"at_s0{i}", [128, T]) for i in range(2)]
            twp = [P.dma("s", wp_sb[:, kc * D:(kc + 1) * D], ds["wproj_b"].ap()[kc]) for kc in range(8)]
            tfm = [t_fm[h] for h in range(NH)]
            tpj = {}
            for mt in range(8):
                par = mt % 2
                pp = ps_av if par == 0 else ps_sc

                def pjmm(t, mt=mt, pp=pp):
                    for kc in range(8):
                        for n0, nn in nsplit(T, 512):
                            mm = t.matmul(pp[:, n0:n0 + nn],
                                          wp_sb[:, kc * D + mt * 128: kc * D + (mt + 1) * 128],
                                          attn_fm[:, kc * T + n0: kc * T + n0 + nn],
                                          start=(kc == 0), stop=(kc == 7))
                    return mm
                tmm = P.run("t", pjmm, deps=tfm + twp + [tpj.get(mt - 2)])
                tev = P.run("v", lambda v, mt=mt, par=par, pp=pp: v.tensor_scalar_add(
                    s0st[par][:], pp[:], bproj_sb[:, mt:mt + 1]), deps=[tmm])
                tpj[mt] = P.dma("a", ds["s0_s"].ap()[s, mt], s0st[par][:], deps=[tev])
        P.barrier()

        CH = min(256, T)
        TPC = CH // 128
        NCH = T // CH
        with ExitStack() as px:
            XS = lambda n, sh, d=F32: px.enter_context(nc.sbuf_tensor(U(n), sh, d))
            xcl = [[XS(f"b_xc{p}{i}", [128, D]) for i in range(TPC)] for p in range(2)]
            cen = XS("b_cen", [128, D])
            sc2 = XS("b_sc2", [128, D])
            stt = XS("b_stt", [128, 12])
            sc2b = XS("b_sc2b", [128, D])
            sttb = XS("b_sttb", [128, 12])
            cenb = XS("b_cenb", [128, D])
            q1 = XS("b_q1", [128, INNER])
            qi = XS("b_qi", [128, INNER], mybir.dt.int32)
            xqb = [XS(f"b_xqb{p}", [128, D], BF16) for p in range(2)]
            xq1T = [XS(f"b_xq1T{p}", [128, 8 * CH], BF16) for p in range(2)]
            ws_g = XS("b_wsg", [128, 8 * 1024], BF16)
            ws_u = XS("b_wsu", [128, 8 * 1024], BF16)
            wsd = [XS(f"b_wsd{i}", [128, D], BF16) for i in range(2)]
            t2sb = XS("b_t2", [128, 32 * CH], BF16)
            t2T = [XS(f"b_t2T{i}", [128, INNER], BF16) for i in range(2)]
            xq2T = XS("b_xq2T", [128, 32 * CH], BF16)
            gbuf = XS("b_gb", [128, 2 * CH])
            g0 = XS("b_g0", [128, 8 * CH])
            s0sl = XS("b_s0", [128, CH])
            rml = XS("b_rm", [128, CH])
            xn1f = XS("b_xn1f", [128, D])
            xnb = XS("b_xnb", [128, D], BF16)
            stgT = XS("b_stgT", [128, 8 * CH], BF16)
            stgT2 = XS("b_stgT2", [128, 8 * CH], BF16)

            t_t2 = {}
            tdmm = {}
            tg0_d = {}
            q1st = {}
            q2st = {}
            gu_l = {}
            cb_l = {}
            trm_l = {}
            LQ = [None]
            PF = [[]]

            def stage_q1(cc):
                p = cc % 2
                res = {"tq1T": []}
                with ExitStack() as pxa:
                    ps_trA = [pxa.enter_context(nc.psum_tensor(U(f"b_pq1_{i}"), [128, 128], BF16))
                              for i in range(2)]
                    trev = {}
                    lt = None
                    res["lnend"] = {}
                    for tto in range(TPC):
                        tt = cc * TPC + tto
                        tx = P.dma("s", xcl[p][tto][:],
                                   din["L0_attn_x"].ap()[s, tt * 128:(tt + 1) * 128, :],
                                   deps=[LQ[0]] + cb_l.get(cc - 2, []))
                        tln = ln_chain(P, xcl[p][tto], cen, sc2, stt, [tx, LQ[0]])
                        res["lnend"][tto] = tln
                        tqd = quant_chain(P, cen, q1, qi, xqb[p], D, stt,
                                          [tln, q1st.get(cc - 2, {}).get("trl")])
                        LQ[0] = tqd
                        for j in range(8):
                            u = tto * 8 + j
                            ttr = P.run("t", lambda t, ps=ps_trA, p=p, j=j, u=u: t.transpose(
                                ps[u % 2][:], xqb[p][:, j * 128:(j + 1) * 128], eye_b[:]),
                                deps=[tqd, trev.get(u - 2)] + (PF[0] if u < 2 else []))
                            trev[u] = P.run("v", lambda v, ps=ps_trA, p=p, j=j, tto=tto, u=u: v.tensor_copy(
                                xq1T[p][:, j * CH + tto * 128: j * CH + (tto + 1) * 128], ps[u % 2][:]),
                                deps=[ttr, gu_l.get(cc - 2)])
                            res["tq1T"].append(trev[u])
                            lt = ttr
                    res["trl"] = lt
                PF[0] = [res["tq1T"][-1]]
                q1st[cc] = res

            def stage_gu(cc):
                p = cc % 2
                tq1T = q1st[cc]["tq1T"]
                with ExitStack() as pxa:
                    ps_g = [pxa.enter_context(nc.psum_tensor(U(f"b_pg{i}"), [128, CH], F32)) for i in range(2)]
                    ps_u = [pxa.enter_context(nc.psum_tensor(U(f"b_pu{i}"), [128, CH], F32)) for i in range(2)]
                    te1 = None
                    for og in range(4):
                        twg = [P.dma("s", ws_g[:, kc * 1024:(kc + 1) * 1024],
                                     ds["wqT_g"].ap()[kc, :, og * 1024:(og + 1) * 1024],
                                     deps=[t_t2.get(cc * 32 + (og - 1) * 8 + 7)]) for kc in range(8)]
                        twu = [P.dma("s", ws_u[:, kc * 1024:(kc + 1) * 1024],
                                     ds["wqT_u"].ap()[kc, :, og * 1024:(og + 1) * 1024],
                                     deps=[t_t2.get(cc * 32 + (og - 1) * 8 + 7)]) for kc in range(8)]
                        for m8 in range(8):
                            mt = og * 8 + m8
                            fk = cc * 32 + mt
                            par = mt % 2

                            def gmm(t, which, m8=m8, par=par, ps_g=ps_g, ps_u=ps_u, p=p):
                                ws = ws_g if which == 0 else ws_u
                                pp = ps_g[par] if which == 0 else ps_u[par]
                                for kc in range(8):
                                    mm = t.matmul(pp[:, 0:CH],
                                                  ws[:, kc * 1024 + m8 * 128: kc * 1024 + (m8 + 1) * 128],
                                                  xq1T[p][:, kc * CH:(kc + 1) * CH],
                                                  start=(kc == 0), stop=(kc == 7))
                                return mm
                            tg = P.run("t", lambda t, g=gmm: g(t, 0),
                                       deps=tq1T + twg + [t_t2.get(fk - 2)] +
                                            (PF[0] if mt == 0 else []))
                            tu = P.run("t", lambda t, g=gmm: g(t, 1),
                                       deps=twu + [t_t2.get(fk - 2)])
                            te1 = P.run("a", lambda a, par=par, ps_g=ps_g: a.activation(
                                gbuf[:, par * CH:(par + 1) * CH], ps_g[par][:, 0:CH], ACT.Silu,
                                scale=rsw_sb[:, 0:1]), deps=[tg, t_t2.get(fk - 2)])
                            t_t2[fk] = P.run("v", lambda v, mt=mt, par=par, ps_u=ps_u: v.tensor_mul(
                                t2sb[:, mt * CH:(mt + 1) * CH], gbuf[:, par * CH:(par + 1) * CH],
                                ps_u[par][:, 0:CH]),
                                deps=[te1, tu, q2st.get(cc - 1, {}).get("tr1l")])
                    gu_l[cc] = tu
                PF[0] = [t_t2[cc * 32 + 31], te1]

            def stage_q2(cc):
                res = {"tq2T": []}
                t2rdy = t_t2[cc * 32 + 31]
                prev = q2st.get(cc - 1, {})
                with ExitStack() as pxa:
                    ps_trB = [pxa.enter_context(nc.psum_tensor(U(f"b_pq2_{i}"), [128, 128], BF16))
                              for i in range(2)]
                    trev = {}
                    u = 0
                    tev_t = {}
                    for tto in range(TPC):
                        tev = []
                        for ot in range(32):
                            ttr = P.run("t", lambda t, ps=ps_trB, ot=ot, tto=tto, u=u: t.transpose(
                                ps[u % 2][:], t2sb[:, ot * CH + tto * 128: ot * CH + (tto + 1) * 128],
                                eye_b[:]),
                                deps=[t2rdy, trev.get(u - 2), prev.get("tr2l")] +
                                     (PF[0] if u < 2 else []))
                            trev[u] = P.run("v", lambda v, ps=ps_trB, ot=ot, tto=tto, u=u: v.tensor_copy(
                                t2T[tto][:, ot * 128:(ot + 1) * 128], ps[u % 2][:]),
                                deps=[ttr, LQ[0], prev.get("tr2l")])
                            tev.append(trev[u])
                            u += 1
                        res["tr1l"] = ttr
                        tev_t[tto] = tev
                    for tto in range(TPC):
                        tqd = quant_chain(P, t2T[tto], q1, qi, t2T[tto], INNER, stt,
                                          tev_t[tto] + [LQ[0]])
                        LQ[0] = tqd
                        for j in range(32):
                            ttr = P.run("t", lambda t, ps=ps_trB, tto=tto, j=j, u=u: t.transpose(
                                ps[u % 2][:], t2T[tto][:, j * 128:(j + 1) * 128], eye_b[:]),
                                deps=[tqd, trev.get(u - 2)])
                            trev[u] = P.run("v", lambda v, ps=ps_trB, j=j, tto=tto, u=u: v.tensor_copy(
                                xq2T[:, j * CH + tto * 128: j * CH + (tto + 1) * 128], ps[u % 2][:]),
                                deps=[ttr, tdmm.get((cc - 1) * 32 + 31)])
                            res["tq2T"].append(trev[u])
                            u += 1
                        res["tr2l"] = ttr
                PF[0] = [res["tq2T"][-1]]
                q2st[cc] = res

            def stage_dn(cc):
                tq2T = q2st[cc]["tq2T"]
                with ExitStack() as pxa:
                    ps8 = [pxa.enter_context(nc.psum_tensor(U(f"b_p8{i}"), [128, CH], F32)) for i in range(8)]
                    for kc in range(32):
                        fk = cc * 32 + kc
                        twd = P.dma("s", wsd[kc % 2][:], ds["wqT_d"].ap()[kc],
                                    deps=[tdmm.get(fk - 2)] + (tq2T if kc < 2 else []))

                        def dmm(t, kc=kc, ps8=ps8):
                            for mt in range(8):
                                mm = t.matmul(ps8[mt][:, 0:CH],
                                              wsd[kc % 2][:, mt * 128:(mt + 1) * 128],
                                              xq2T[:, kc * CH:(kc + 1) * CH],
                                              start=(kc == 0), stop=(kc == 31))
                            return mm
                        tdmm[fk] = P.run("t", dmm, deps=[twd] + tq2T +
                                         ([trm_l.get(cc - 1)] + PF[0] if kc == 0 else []))
                    tg0_d[cc] = [P.run("a", lambda a, mt=mt, ps8=ps8: a.activation(
                        g0[:, mt * CH:(mt + 1) * CH], ps8[mt][:, 0:CH], ACT.Silu,
                        scale=rsw_sb[:, 4:5]), deps=[tdmm[cc * 32 + 31], trm_l.get(cc - 1)])
                        for mt in range(8)]
                PF[0] = [tg0_d[cc][-1]]

            def stage_cb(cc):
                p = cc % 2
                prev_end = cb_l.get(cc - 1, [])
                with ExitStack() as pxa:
                    ps_tr = [pxa.enter_context(nc.psum_tensor(U(f"b_pc{i}"), [128, 128], F32)) for i in range(2)]
                    ps_trb = [pxa.enter_context(nc.psum_tensor(U(f"b_pcb{i}"), [128, 128], BF16)) for i in range(2)]
                    u = 0
                    ub = 0
                    trev = {}
                    trevb = {}
                    tx1 = {tto: [] for tto in range(TPC)}
                    trm = None
                    for mt in range(8):
                        ts0 = P.dma("s", s0sl[:], ds["s0_s"].ap()[s, mt][:, cc * CH:(cc + 1) * CH],
                                    deps=[trm] + (prev_end if mt == 0 else []))
                        trm = P.run("v", lambda v, mt=mt: v.tensor_mul(rml[:], s0sl[:],
                                                                       g0[:, mt * CH:(mt + 1) * CH]),
                                    deps=[ts0, tg0_d[cc][mt]])
                        for tto in range(TPC):
                            ttr = P.run("t", lambda t, ps=ps_tr, tto=tto, u=u: t.transpose(
                                ps[u % 2][:], rml[:, tto * 128:(tto + 1) * 128], eye_f[:]),
                                deps=[trm, trev.get(u - 2)] + (PF[0] if u < 2 else []))
                            trev[u] = P.run("v", lambda v, ps=ps_tr, p=p, mt=mt, tto=tto, u=u: v.tensor_add(
                                xcl[p][tto][:, mt * 128:(mt + 1) * 128], xcl[p][tto][:, mt * 128:(mt + 1) * 128],
                                ps[u % 2][:]), deps=[ttr, q1st[cc]["lnend"][tto]] +
                                (prev_end if u == 0 else []))
                            tx1[tto].append(trev[u])
                            u += 1
                    trm_l[cc] = trm
                    t_end = []
                    lnp = None
                    for tto in range(TPC):
                        tt = cc * TPC + tto
                        tst = P.dma("s", ds["x1_s"].ap()[s, tt * 128:(tt + 1) * 128, :], xcl[p][tto][:],
                                    deps=tx1[tto])
                        tln = ln_chain(P, xcl[p][tto], xn1f, sc2b, sttb, tx1[tto] + [lnp])
                        tcast = P.run("v", lambda v: v.tensor_copy(xnb[:], xn1f[:]), deps=[tln])
                        for j in range(8):
                            ttr = P.run("t", lambda t, j=j, ub=ub: t.transpose(
                                ps_trb[ub % 2][:], xnb[:, j * 128:(j + 1) * 128], eye_b[:]),
                                deps=[tcast, trevb.get(ub - 2)])
                            trevb[ub] = P.run("a", lambda a, j=j, tto=tto, ub=ub: a.activation(
                                stgT[:, j * CH + tto * 128: j * CH + (tto + 1) * 128],
                                ps_trb[ub % 2][:], ACT.Identity), deps=[ttr])
                            t_end.append(trevb[ub])
                            ub += 1
                        tln2 = ln_chain(P, xn1f, cenb, sc2b, sttb, [tcast])
                        lnp = tln2
                        tcast2 = P.run("v", lambda v: v.tensor_copy(xnb[:], cenb[:]),
                                       deps=[tln2, ttr])
                        for j in range(8):
                            ttr = P.run("t", lambda t, j=j, ub=ub: t.transpose(
                                ps_trb[ub % 2][:], xnb[:, j * 128:(j + 1) * 128], eye_b[:]),
                                deps=[tcast2, trevb.get(ub - 2)])
                            trevb[ub] = P.run("a", lambda a, j=j, tto=tto, ub=ub: a.activation(
                                stgT2[:, j * CH + tto * 128: j * CH + (tto + 1) * 128],
                                ps_trb[ub % 2][:], ACT.Identity,
                                bias=bet_sb[:, j:j + 1], scale=gam_sb[:, j:j + 1]), deps=[ttr])
                            t_end.append(trevb[ub])
                            ub += 1
                        t_end.append(tst)
                    tsp1 = [P.dma("s", ds["xn1T_s"].ap()[s, j, :, cc * CH:(cc + 1) * CH],
                                  stgT[:, j * CH:(j + 1) * CH], deps=t_end) for j in range(8)]
                    tsp2 = [P.dma("s", ds["xniT_s"].ap()[s, j, :, cc * CH:(cc + 1) * CH],
                                  stgT2[:, j * CH:(j + 1) * CH], deps=t_end) for j in range(8)]
                    cb_l[cc] = tsp1 + tsp2
                PF[0] = [trev[u - 1], trevb[ub - 1]]

            stage_q1(0)
            stage_gu(0)
            if NCH > 1:
                stage_q1(1)
            for cc in range(NCH):
                stage_q2(cc)
                stage_dn(cc)
                if cc + 1 < NCH:
                    stage_gu(cc + 1)
                stage_cb(cc)
                if cc + 2 < NCH:
                    stage_q1(cc + 2)
        P.barrier()

        with ExitStack() as px:
            XS = lambda n, sh, d=F32: px.enter_context(nc.sbuf_tensor(U(n), sh, d))
            xniT = XS("c_xniT", [128, 8 * T], BF16)
            ballT = XS("c_ball", [128, T])
            apw = XS("c_apw", [128, T])
            asf = XS("c_asf", [128, T])
            dT = XS("c_dT", [128, T])
            cum = XS("c_cum", [128, T])
            stb = XS("c_stb", [128, T], BF16)
            ssm = XS("c_ssm", [128, 8 * T], BF16)
            pxT = XS("c_pxT", [128, 8 * G3], BF16)
            pst = [XS(f"c_pst{i}", [128, T]) for i in range(2)]
            onesT = XS("c_onesT", [128, T])
            ps1 = px.enter_context(nc.psum_tensor(U("c_ps1"), [128, T], F32))
            ps2 = px.enter_context(nc.psum_tensor(U("c_ps2"), [128, T], F32))

            txn = [P.dma("s", xniT[:, j * T:(j + 1) * T], ds["xniT_s"].ap()[s, j]) for j in range(8)]
            tap = P.dma("s", apw[:], ds["apow_d"].ap())
            tas = P.dma("s", asf[:], ds["asafe_d"].ap())

            def bmm(t):
                for kc in range(8):
                    for n0, nn in nsplit(T, 512):
                        mm = t.matmul(ps1[:, n0:n0 + nn], BwT_sb[:, kc * 128:(kc + 1) * 128],
                                      xniT[:, kc * T + n0: kc * T + n0 + nn],
                                      start=(kc == 0), stop=(kc == 7))
                return mm
            tb1 = P.run("t", bmm, deps=txn)
            tb2 = P.run("v", lambda v: v.tensor_scalar_add(ballT[:], ps1[:], Bb_sb[:, 0:1]),
                        deps=[tb1])
            t_on = P.run("v", lambda v: v.memset(onesT[:], 1.0))
            tdr = P.run("v", lambda v: v.reciprocal(dT[:], asf[:]), deps=[tas])
            td = P.run("v", lambda v: v.tensor_mul(dT[:], ballT[:], dT[:]), deps=[tb2, tdr])
            tc = td
            for c0, cn in nsplit(T, 512):
                init = 0.0 if c0 == 0 else cum[:, c0 - 1:c0]
                tc = P.run("v", lambda v, c0=c0, cn=cn, init=init: v.tensor_tensor_scan(
                    cum[:, c0:c0 + cn], onesT[:, c0:c0 + cn],
                    dT[:, c0:c0 + cn], init, ALU.mult, ALU.add), deps=[tc, t_on])
            tst = P.run("v", lambda v: v.tensor_mul(stb[:], apw[:], cum[:]), deps=[tc, tap])
            tssm = []
            for mt in range(8):
                def cmm(t, mt=mt):
                    for n0, nn in nsplit(T, 512):
                        mm = t.matmul(ps2[:, n0:n0 + nn], CwT_sb[:, mt * 128:(mt + 1) * 128],
                                      stb[:, n0:n0 + nn], start=True, stop=True)
                    return mm
                tm2 = P.run("t", cmm, deps=[tst, tssm[-1] if tssm else None])
                tssm.append(P.run("v", lambda v, mt=mt: v.tensor_scalar_add(
                    ssm[:, mt * T:(mt + 1) * T], ps2[:], Cb_sb[:, mt:mt + 1]),
                    deps=[tm2]))
            tpx = [P.dma("s", pxT[:, kc * G3:(kc + 1) * G3], ds["PxT_s"].ap()[kc]) for kc in range(8)]
            tpj = {}
            for mt in range(NM):
                par = mt % 2
                pp = ps1 if par == 0 else ps2

                def pmm(t, mt=mt, pp=pp):
                    for kc in range(8):
                        for n0, nn in nsplit(T, 512):
                            mm = t.matmul(pp[:, n0:n0 + nn],
                                          pxT[:, kc * G3 + mt * 128: kc * G3 + (mt + 1) * 128],
                                          ssm[:, kc * T + n0: kc * T + n0 + nn],
                                          start=(kc == 0), stop=(kc == 7))
                    return mm
                tmm = P.run("t", pmm, deps=tssm + tpx + [tpj.get(mt - 2), tb2 if mt == 0 else None,
                                                         tssm[-1] if mt == 1 else None])
                tev = P.run("v", lambda v, mt=mt, par=par, pp=pp: v.tensor_scalar_add(
                    pst[par][:], pp[:], Pb_sb[:, mt:mt + 1]), deps=[tmm])
                tpj[mt] = P.dma("a", ds["P_s"].ap()[:, mt, s, :], pst[par][:], deps=[tev])
        P.barrier()

    with ExitStack() as px:
        XS = lambda n, sh, d=F32: px.enter_context(nc.sbuf_tensor(U(n), sh, d))
        vh = XS("d_vh", [128, 4 * G3], BF16)
        cfc = XS("d_cfc", [128, T * S4], BF16)
        pbuf = [XS(f"d_pb{i}", [128, NM * SEQ * 128]) for i in range(2)]
        sg = XS("d_sg", [128, NM * SEQ])
        dd = XS("d_dd", [128, 4 * SEQ])
        tm2 = XS("d_tm", [128, 4 * SEQ])
        pp2 = [px.enter_context(nc.psum_tensor(U(f"d_pp{i}"), [128, NM * SEQ], F32)) for i in range(2)]

        tvh = [P.dma("s", vh[:, kc * G3:(kc + 1) * G3], ds["VhT_s"].ap()[kc]) for kc in range(4)]
        NCHK = T // 128
        tpd = {}
        t_t = {}
        tpd[0] = P.dma("s", pbuf[0][:].rearrange("p (m b t) -> p m b t", m=NM, b=SEQ),
                       ds["P_s"].ap()[:, :, :, 0:128])
        if NCHK > 1:
            tpd[1] = P.dma("s", pbuf[1][:].rearrange("p (m b t) -> p m b t", m=NM, b=SEQ),
                           ds["P_s"].ap()[:, :, :, 128:256])
        t_v = {}
        t_a = {}
        t_z = {}

        def pcopy(st):
            pq = pp2[st % 2]
            prhs = pbuf[(st // 128) % 2][:].rearrange("p (m b t) -> p m b t",
                                                      m=NM, b=SEQ)[:, :, :, st % 128]
            return lambda v, pq=pq, prhs=prhs: v.tensor_copy(
                pq[:].rearrange("p (m b) -> p m b", m=NM), prhs)
        t_z[0] = P.run("v", pcopy(0), deps=[tpd[0]])
        t_z[1] = P.run("v", pcopy(1), deps=[tpd[0]])
        for st in range(T):
            chunk = st // 128
            pp = pp2[st % 2]

            def step_mm(t, st=st, pp=pp):
                for m in range(NM):
                    for kc in range(4):
                        if st == 0:
                            rhs = h0_sb[:, kc * SEQ:(kc + 1) * SEQ]
                        else:
                            rhs = cfc[:, (st - 1) * S4 + kc * SEQ: (st - 1) * S4 + (kc + 1) * SEQ]
                        mm = t.matmul(pp[:, m * SEQ:(m + 1) * SEQ],
                                      vh[:, kc * G3 + m * 128: kc * G3 + (m + 1) * 128],
                                      rhs, start=False, stop=False,
                                      skip_group_check=True)
                return mm
            t_t[st] = P.run("t", step_mm,
                            deps=[t_v.get(st - 1), t_z[st]] + (tvh if st == 0 else []))
            t_a[st] = P.run("a", lambda a, pp=pp: a.activation(sg[:], pp[:], ACT.Tanh),
                            deps=[t_t[st]])

            def step_comb(v, st=st):
                v.tensor_sub(dd[:], sg[:, 4 * SEQ:8 * SEQ], sg[:, 0:4 * SEQ])
                v.drain()
                v.scalar_tensor_tensor(tm2[:], sg[:, 8 * SEQ:12 * SEQ], 1.0, dd[:],
                                       ALU.add, ALU.mult)
                v.drain()
                return v.scalar_tensor_tensor(cfc[:, st * S4:(st + 1) * S4], tm2[:], 0.5,
                                              sg[:, 0:4 * SEQ], ALU.mult, ALU.add)
            t_v[st] = P.run("v", step_comb, deps=[t_a[st]])
            if st + 2 < T:
                t_z[st + 2] = P.run("v", pcopy(st + 2),
                                    deps=[t_a[st], tpd[(st + 2) // 128]])
            if st % 128 == 127 and chunk + 2 < NCHK:
                tpd[chunk + 2] = P.dma("s",
                                       pbuf[chunk % 2][:].rearrange("p (m b t) -> p m b t", m=NM, b=SEQ),
                                       ds["P_s"].ap()[:, :, :, (chunk + 2) * 128:(chunk + 3) * 128],
                                       deps=[t_z[st]])
        tcs = P.dma("s", ds["cfc_s"].ap(), cfc[:], deps=[t_v[T - 1]])
    P.barrier()

    for s in range(SEQ):
        with ExitStack() as px:
            XS = lambda n, sh, d=F32: px.enter_context(nc.sbuf_tensor(U(n), sh, d))
            cfc = XS("e_cfc", [128, T * S4], BF16)
            owT = XS("e_ow", [128, 4 * D], BF16)
            ust = XS("e_ust", [128, T])
            stg = [XS(f"e_stg{i}", [128, 128]) for i in range(2)]
            tcf = P.dma("s", cfc[:], ds["cfc_s"].ap())
            tow = [P.dma("s", owT[:, kc * D:(kc + 1) * D], ds["outwT_s"].ap()[kc]) for kc in range(4)]
            cfr = cfc[:].rearrange("p (t g) -> p t g", g=S4)
            with ExitStack() as pxa:
                ps_u = pxa.enter_context(nc.psum_tensor(U("e_psu"), [128, T], F32))
                ps_tr = [pxa.enter_context(nc.psum_tensor(U(f"e_ptr{i}"), [128, 128], F32)) for i in range(2)]
                u = 0
                tpc = {}
                for mt in range(8):
                    def umm(t, mt=mt):
                        for kc in range(4):
                            for n0, nn in nsplit(T, 512):
                                mm = t.matmul(ps_u[:, n0:n0 + nn],
                                              owT[:, kc * D + mt * 128: kc * D + (mt + 1) * 128],
                                              cfr[:, n0:n0 + nn, kc * SEQ + s],
                                              start=(kc == 0), stop=(kc == 3))
                        return mm
                    tmm = P.run("t", umm, deps=[tcf] + tow + [tpc.get(u - 1)])
                    tev = P.run("v", lambda v, mt=mt: v.tensor_scalar_add(ust[:], ps_u[:],
                                                                           outb_sb[:, mt:mt + 1]), deps=[tmm])
                    for tt in range(TT):
                        ttr = P.run("t", lambda t, tt=tt, u=u: t.transpose(
                            ps_tr[u % 2][:], ust[:, tt * 128:(tt + 1) * 128], eye_f[:]),
                            deps=[tev, tpc.get(u - 2)])
                        tec = P.run("v", lambda v, u=u: v.tensor_copy(stg[u % 2][:], ps_tr[u % 2][:]), deps=[ttr])
                        tpc[u] = P.dma("a", ds["uT_s"].ap()[s, tt * 128:(tt + 1) * 128,
                                                            mt * 128:(mt + 1) * 128],
                                       stg[u % 2][:], deps=[tec])
                        u += 1
        P.barrier()

        CH2 = min(512, T)
        NCH2 = T // CH2
        with ExitStack() as px:
            XS = lambda n, sh, d=F32: px.enter_context(nc.sbuf_tensor(U(n), sh, d))
            xn1T = XS("e2_xn1T", [128, 8 * T], BF16)
            wsl = XS("e2_wsl", [128, 8 * 1024], BF16)
            wsl2 = [XS(f"e2_wsl2{i}", [128, 32 * 512], BF16) for i in range(2)]
            wsd = [XS(f"e2_wsd{i}", [128, D], BF16) for i in range(2)]
            r1 = XS("e2_r1", [128, 32 * CH2], BF16)
            r2 = XS("e2_r2", [128, 32 * CH2], BF16)
            g1c = XS("e2_g1c", [128, 8 * CH2])
            stg = [XS(f"e2_stg{i}", [128, 128]) for i in range(2)]
            txn = [P.dma("s", xn1T[:, j * T:(j + 1) * T], ds["xn1T_s"].ap()[s, j]) for j in range(8)]
            prev_end = None
            for cc in range(NCH2):
                c0 = cc * CH2
                t_r1 = {}
                t_r2 = {}
                with ExitStack() as pxa:
                    ps_h = [pxa.enter_context(nc.psum_tensor(U(f"e2_ph{i}"), [128, CH2], F32)) for i in range(2)]
                    for og in range(4):
                        twl = [P.dma("s", wsl[:, kc * 1024:(kc + 1) * 1024],
                                     ds["A0_s"].ap()[kc, :, og * 1024:(og + 1) * 1024],
                                     deps=[t_r1.get(og * 8 - 1), prev_end]) for kc in range(8)]
                        for m8 in range(8):
                            mt = og * 8 + m8
                            par = mt % 2

                            def h1mm(t, m8=m8, par=par, c0=c0):
                                for kc in range(8):
                                    for n0, nn in nsplit(CH2, 512):
                                        mm = t.matmul(ps_h[par][:, n0:n0 + nn],
                                                      wsl[:, kc * 1024 + m8 * 128: kc * 1024 + (m8 + 1) * 128],
                                                      xn1T[:, kc * T + c0 + n0: kc * T + c0 + n0 + nn],
                                                      start=(kc == 0), stop=(kc == 7))
                                return mm
                            tmm = P.run("t", h1mm, deps=txn + twl + [t_r1.get(mt - 2)])
                            t_r1[mt] = P.run("a", lambda a, mt=mt, par=par: a.activation(
                                r1[:, mt * CH2:(mt + 1) * CH2], ps_h[par][:], ACT.Relu,
                                scale=sc_sb[:, 0:1]), deps=[tmm])
                    t_r1all = [t_r1[mt] for mt in range(32)]
                    for og in range(8):
                        wpar = og % 2
                        twl2 = [P.dma("s", wsl2[wpar][:, kc * 512:(kc + 1) * 512],
                                      ds["A1_s"].ap()[kc, :, og * 512:(og + 1) * 512],
                                      deps=[t_r2.get((og - 2) * 4 + 3)]) for kc in range(32)]
                        for m4 in range(4):
                            mt = og * 4 + m4
                            par = mt % 2

                            def h2mm(t, m4=m4, par=par, wpar=wpar):
                                for kc in range(32):
                                    for n0, nn in nsplit(CH2, 512):
                                        mm = t.matmul(ps_h[par][:, n0:n0 + nn],
                                                      wsl2[wpar][:, kc * 512 + m4 * 128: kc * 512 + (m4 + 1) * 128],
                                                      r1[:, kc * CH2 + n0: kc * CH2 + n0 + nn],
                                                      start=(kc == 0), stop=(kc == 31))
                                return mm
                            tmm = P.run("t", h2mm, deps=t_r1all + twl2 + [t_r2.get(mt - 2)])
                            t_r2[mt] = P.run("a", lambda a, mt=mt, par=par: a.activation(
                                r2[:, mt * CH2:(mt + 1) * CH2], ps_h[par][:], ACT.Relu,
                                scale=sc_sb[:, 1:2]), deps=[tmm])
                    t_r2all = [t_r2[mt] for mt in range(32)]
                tg1 = []
                with ExitStack() as pxa:
                    ps8 = [pxa.enter_context(nc.psum_tensor(U(f"e2_p8{i}"), [128, CH2], F32)) for i in range(8)]
                    twd = {}
                    tdmm = {}
                    for kc in range(32):
                        twd[kc] = P.dma("s", wsd[kc % 2][:], ds["A2_s"].ap()[kc],
                                        deps=[tdmm.get(kc - 2)] + (t_r2all if kc < 2 else []))

                        def dmm(t, kc=kc):
                            for mt in range(8):
                                for n0, nn in nsplit(CH2, 512):
                                    mm = t.matmul(ps8[mt][:, n0:n0 + nn],
                                                  wsd[kc % 2][:, mt * 128:(mt + 1) * 128],
                                                  r2[:, kc * CH2 + n0: kc * CH2 + n0 + nn],
                                                  start=(kc == 0), stop=(kc == 31))
                            return mm
                        tdmm[kc] = P.run("t", dmm, deps=[twd[kc]] + t_r2all)
                    tg1 = [P.run("a", lambda a, mt=mt: a.activation(
                        g1c[:, mt * CH2:(mt + 1) * CH2], ps8[mt][:], ACT.Silu,
                        scale=rsw_sb[:, 3:4]), deps=[tdmm[31]]) for mt in range(8)]
                with ExitStack() as pxa:
                    ps_tr = [pxa.enter_context(nc.psum_tensor(U(f"e2_ptr{i}"), [128, 128], F32)) for i in range(2)]
                    u = 0
                    tpc = {}
                    for mt in range(8):
                        for tto in range(CH2 // 128):
                            ttr = P.run("t", lambda t, mt=mt, tto=tto, u=u: t.transpose(
                                ps_tr[u % 2][:], g1c[:, mt * CH2 + tto * 128: mt * CH2 + (tto + 1) * 128],
                                eye_f[:]), deps=[tg1[mt], tpc.get(u - 2)] + (tg1 if u < 2 else []))
                            tec = P.run("v", lambda v, u=u: v.tensor_copy(stg[u % 2][:], ps_tr[u % 2][:]),
                                        deps=[ttr])
                            tt_abs = c0 + tto * 128
                            tpc[u] = P.dma("a", ds["g1T_s"].ap()[s, tt_abs:tt_abs + 128,
                                                                 mt * 128:(mt + 1) * 128],
                                           stg[u % 2][:], deps=[tec])
                            u += 1
                    prev_end = tpc[u - 1]
                P.barrier()
        P.barrier()

        with ExitStack() as px:
            XS = lambda n, sh, d=F32: px.enter_context(nc.sbuf_tensor(U(n), sh, d))
            x1l = [XS(f"e3_x1{i}", [128, D]) for i in range(2)]
            utl = [XS(f"e3_ut{i}", [128, D]) for i in range(2)]
            g1l = [XS(f"e3_g1{i}", [128, D]) for i in range(2)]
            cen = XS("e3_cen", [128, D])
            sc2 = XS("e3_sc2", [128, D])
            stt = XS("e3_stt", [128, 8])
            rr = [XS(f"e3_r{i}", [128, D]) for i in range(2)]
            t_done = {}
            for tt in range(TT):
                par = tt % 2
                dp = [t_done.get(tt - 2)]
                tl1 = P.dma("s", x1l[par][:], ds["x1_s"].ap()[s, tt * 128:(tt + 1) * 128, :], deps=dp)
                tl2 = P.dma("s", utl[par][:], ds["uT_s"].ap()[s, tt * 128:(tt + 1) * 128, :], deps=dp)
                tl3 = P.dma("s", g1l[par][:], ds["g1T_s"].ap()[s, tt * 128:(tt + 1) * 128, :], deps=dp)
                tln = ln_chain(P, x1l[par], cen, sc2, stt, [tl1, t_done.get(tt - 2)])
                ts1 = P.run("v", lambda v, par=par: v.tensor_add(cen[:], cen[:], utl[par][:]),
                            deps=[tln, tl2])
                tr = P.run("v", lambda v, par=par: v.tensor_mul(rr[par][:], cen[:], g1l[par][:]),
                           deps=[ts1, tl3])
                tx2 = P.run("v", lambda v, par=par: v.tensor_add(rr[par][:], rr[par][:], x1l[par][:]),
                            deps=[tr])
                t_done[tt] = P.dma("s", out_h.ap()[s, tt * 128:(tt + 1) * 128, :], rr[par][:],
                                   deps=[tx2])
            P.wait("s", [t_done[TT - 1]])
        P.barrier()

    ctx.close()
    return nc, din, ds, out_h


_NC_CACHE = {}


def kernel(**inputs):
    SEQ = 16 // NCORES
    T = 2048
    key = (SEQ, T)
    if key not in _NC_CACHE:
        _NC_CACHE[key] = build(SEQ, T)
    nc, din, ds, out_h = _NC_CACHE[key]
    base = {}
    for name, arr in inputs.items():
        a = np.ascontiguousarray(arr)
        if a.dtype == np.bool_:
            a = a.view(np.uint8)
        if name in ("L1_ffn_s0", "L1_ffn_s1", "L1_ffn_s2"):
            a = a.reshape(1, 1)
        base[name] = a
    x = base.pop("L0_attn_x")
    in_maps = []
    for c in range(NCORES):
        m = dict(base)
        m["L0_attn_x"] = x[c * SEQ:(c + 1) * SEQ]
        in_maps.append(m)
    res = run_bass_kernel_spmd(nc, in_maps, core_ids=list(range(NCORES)))
    return np.concatenate([r["out"] for r in res.results], axis=0)
